# version 11
# baseline (speedup 1.0000x reference)
import contextlib
import numpy as np
import ml_dtypes
import concourse.bass as bass
import concourse.mybir as mybir
from concourse.bass_utils import run_bass_kernel_spmd

F32 = mybir.dt.float32
BF16 = mybir.dt.bfloat16
AF = mybir.ActivationFunctionType
ALU = mybir.AluOpType
AX = mybir.AxisListType
BF = ml_dtypes.bfloat16

P = 128
NBIG = -1e30

CFG_FULL = dict(
    D=2048, TOK=4096, TT=512, DTR=256, NS=8, E=16, I=1408, SI=5632,
    NPREV=3, TOPK=4, OVERLAP=True,
    NGU_RING=2, NWD_RING=2, SIM_SYNC=True,
)


def _check(cfg):
    assert cfg["D"] % P == 0 and cfg["TOK"] % cfg["TT"] == 0
    assert cfg["DTR"] % P == 0 and cfg["I"] % P == 0 and cfg["SI"] % cfg["I"] == 0
    assert cfg["TT"] % P == 0 and cfg["NS"] == 8 and cfg["TT"] <= 512


class Builder:
    ENG_SEM = {"tensor": "pe", "vector": "dve", "scalar": "act",
               "gpsimd": "pool", "sync": "sp"}

    def __init__(self, nc, cfg):
        self.nc = nc
        self.cfg = cfg
        self.same_sync = cfg.get("SIM_SYNC", False)
        self.q = {e: [] for e in self.ENG_SEM}
        self.cnt = {}
        self.sems = {}
        self.lastwait = {e: {} for e in self.ENG_SEM}

    def emit(self, eng, fn):
        self.q[eng].append(fn)

    def mksem(self, name, stack):
        h = stack.enter_context(self.nc.semaphore(name))
        self.sems[name] = h
        self.cnt[name] = 0

    def wait(self, eng, s, v):
        if v <= 0 or (self.ENG_SEM[eng] == s and not self.same_sync):
            return
        lw = self.lastwait[eng]
        if lw.get(s, 0) >= v:
            return
        lw[s] = v
        h = self.sems[s]
        self.emit(eng, lambda e, h=h, v=v: e.wait_ge(h, v))

    def waits(self, eng, ws):
        best = {}
        for s, v in ws:
            if v > best.get(s, 0):
                best[s] = v
        for s, v in best.items():
            self.wait(eng, s, v)

    def eop(self, eng, build, waits=(), inc=True):
        s = self.ENG_SEM[eng]
        if self.same_sync:
            self.waits(eng, list(waits) + [(s, self.cnt[s])])
        else:
            self.waits(eng, waits)
        if inc:
            self.cnt[s] += 1
            v = self.cnt[s]
            h = self.sems[s]
            self.emit(eng, lambda e, b=build, h=h: b(e).then_inc(h, 1))
            return (s, v)
        self.emit(eng, lambda e, b=build: b(e))
        return (s, self.cnt[s])

    def dma(self, eng, out_ap, in_ap, slotsem, waits=()):
        self.waits(eng, waits)
        self.cnt[slotsem] += 16
        v = self.cnt[slotsem]
        h = self.sems[slotsem]
        self.emit(eng, lambda e, o=out_ap, i=in_ap, h=h:
                  e.dma_start(out=o, in_=i).then_inc(h, 16))
        return (slotsem, v)

    @staticmethod
    def trk():
        return {"ready": [], "frees": []}

    def produce(self, t, rv):
        t["ready"] = [rv]
        t["frees"] = []

    def consume(self, t, rv):
        t["frees"].append(rv)

    def now(self, *sems):
        return [(s, self.cnt[s]) for s in sems]

    def build(self):
        nc, cfg = self.nc, self.cfg
        D, TOK, TT = cfg["D"], cfg["TOK"], cfg["TT"]
        DTR, NS, E, I, SI = cfg["DTR"], cfg["NS"], cfg["E"], cfg["I"], cfg["SI"]
        NPREV = cfg["NPREV"]
        NC = D // P
        NT = TOK // TT
        NKD = DTR // P
        NU = I // P
        NQ = SI // I
        XK = DTR + 2 * NS
        NGU, NWD = cfg["NGU_RING"], cfg["NWD_RING"]

        dt = nc.dram_tensor
        self.d = d = {}
        d["x_own"] = dt("x_own", [D, TOK], F32, kind="ExternalInput").ap()
        d["x_prev"] = dt("x_prev", [D, NPREV * TOK], F32, kind="ExternalInput").ap()
        d["wxp"] = dt("wxp", [D, XK], BF16, kind="ExternalInput").ap()
        d["wdt"] = dt("wdt", [DTR, D], BF16, kind="ExternalInput").ap()
        d["alog"] = dt("alog", [P, NC * NS], F32, kind="ExternalInput").ap()
        d["bdt"] = dt("bdt", [P, NC], F32, kind="ExternalInput").ap()
        d["dpar"] = dt("dpar", [P, NC], F32, kind="ExternalInput").ap()
        d["gate_l"] = dt("gate_l", [P, NC * E], BF16, kind="ExternalInput").ap()
        d["wgu"] = dt("wgu", [E * 2 * NU, P, NC * P], BF16,
                      kind="ExternalInput").ap()
        d["wdn"] = dt("wdn", [E * (NC // 2), P, NU * 2 * P], BF16,
                      kind="ExternalInput").ap()
        d["sgu"] = dt("sgu", [2 * (SI // P), P, NC * P], BF16,
                      kind="ExternalInput").ap()
        d["sdn"] = dt("sdn", [NQ * (NC // 2), P, NU * 2 * P], BF16,
                      kind="ExternalInput").ap()
        d["sgv_l"] = dt("sgv_l", [P, NC], BF16, kind="ExternalInput").ap()
        d["e16"] = dt("e16", [16, 16 * P], BF16, kind="ExternalInput").ap()
        d["ident"] = dt("ident", [P, P], F32, kind="ExternalInput").ap()
        d["onesb"] = dt("onesb", [P, 1], BF16, kind="ExternalInput").ap()
        d["onesr"] = dt("onesr", [1, P], F32, kind="ExternalInput").ap()
        d["selin"] = dt("selin", [P, 2 * NPREV], F32, kind="ExternalInput").ap()
        d["out_fm"] = dt("out_fm", [D, TOK], F32, kind="ExternalOutput").ap()
        if cfg.get("DEBUG"):
            d["dbg_hin"] = dt("dbg_hin", [P, NC * NS], F32, kind="ExternalOutput").ap()
            d["dbg_sacc"] = dt("dbg_sacc", [P, NC], F32, kind="ExternalOutput").ap()
            for r in range(NPREV):
                d[f"dbg_ar{r}"] = dt(f"dbg_ar{r}", [P, NC * NS], F32, kind="ExternalOutput").ap()
                d[f"dbg_br{r}"] = dt(f"dbg_br{r}", [P, NC * NS], F32, kind="ExternalOutput").ap()
        d["xn_dram"] = dt("xn_dram", [NT, P, NC * TT], BF16).ap()
        d["y_dram"] = dt("y_dram", [D, TOK], F32).ap()
        d["acc_dram"] = dt("acc_dram", [D, TOK], F32).ap()

        st_ctx = contextlib.ExitStack()
        with st_ctx:
            def sb(name, shape, dty):
                return st_ctx.enter_context(nc.sbuf_tensor(name, shape, dty))
            h = self.h = {}
            h["wxp_s"] = sb("wxp_s", [P, NC * XK], BF16)
            h["wdt_s"] = sb("wdt_s", [P, NKD * D], BF16)
            h["gate_s"] = sb("gate_s", [P, NC * E], BF16)
            h["sgv_s"] = sb("sgv_s", [P, NC], BF16)
            h["alog_s"] = sb("alog_s", [P, NC * NS], F32)
            h["bdt_s"] = sb("bdt_s", [P, NC], F32)
            h["d_s"] = sb("d_s", [P, NC], F32)
            h["e16_s"] = sb("e16_s", [16, 16 * P], BF16)
            h["id_s"] = sb("id_s", [P, P], F32)
            h["onesb_s"] = sb("onesb_s", [P, 1], BF16)
            h["onesr_s"] = sb("onesr_s", [1, P], F32)
            h["sel_s"] = sb("sel_s", [P, 2 * NPREV], F32)
            h["cvals"] = sb("cvals", [P, 4], F32)
            h["x_sl"] = [sb(f"x{i}", [P, TT], F32) for i in range(2)]
            h["xbf_sl"] = [sb(f"xbf{i}", [P, TT], BF16) for i in range(2)]
            h["xsq_sl"] = [sb(f"xsq{i}", [P, TT], BF16) for i in range(2)]
            h["xn_t"] = sb("xn_t", [P, NC * TT], BF16)
            h["arep_s"] = sb("arep_s", [P, TT], F32)
            h["brep_s"] = sb("brep_s", [P, TT], F32)
            h["dtr_s"] = sb("dtr_s", [P, NKD * TT], BF16)
            h["bc_s"] = sb("bc_s", [16, TT], BF16)
            h["ball_s"] = sb("ball_s", [P, NS * TT], BF16)
            h["call_s"] = sb("call_s", [P, NS * TT], BF16)
            h["delta_sl"] = [sb(f"delta{i}", [P, TT], F32) for i in range(2)]
            h["sfpA"] = sb("sfpA", [P, TT], F32)
            h["sfpB"] = sb("sfpB", [P, TT], F32)
            h["u_s"] = sb("u_s", [P, TT], F32)
            h["dA_sl"] = [sb(f"dA{i}", [P, TT], F32) for i in range(2)]
            h["dBu_sl"] = [sb(f"dBu{i}", [P, TT], F32) for i in range(2)]
            h["h_all"] = sb("h_all", [P, NS * TT], F32)
            h["y_sl"] = [sb("y0", [P, TT], F32)]
            rows3 = sb("rows3", [1, 3 * TT], F32)
            h["rows"] = {"m": rows3[:, 0:TT], "var": rows3[:, TT:2 * TT],
                         "tmp": rows3[:, 2 * TT:3 * TT]}
            h["spart"] = sb("spart", [P, 1], F32)
            h["sacc"] = sb("sacc", [P, NC], F32)
            h["carry"] = sb("carry", [P, NC * NS], F32)
            h["ar_s"] = [sb(f"ar{r}", [P, NC * NS], F32) for r in range(NPREV)]
            h["br_s"] = [sb(f"br{r}", [P, NC * NS], F32) for r in range(NPREV)]
            h["wgu_sl"] = [sb(f"wgu{i}", [P, (NC // 2) * P], BF16)
                           for i in range(4)]
            h["wd_sl"] = [sb(f"wd{i}", [P, NU * 2 * P], BF16) for i in range(NWD)]
            h["xnf_s"] = sb("xnf_s", [P, NC * TT], BF16)
            h["acc_s"] = sb("acc_s", [P, NC * TT], F32)
            h["h_s"] = sb("h_s", [P, NU * TT], BF16)
            h["silu_s"] = sb("silu_s", [P, TT], BF16)
            h["upsc_s"] = sb("upsc_s", [P, TT], BF16)
            h["cw_s"] = sb("cw_s", [P, TT], BF16)
            h["grep_s"] = sb("grep_s", [P, TT], BF16)
            h["logit_s"] = sb("logit_s", [16, TT], F32)
            NG = TT // P
            h["ltm_s"] = sb("ltm_s", [P, NG * 16], F32)
            h["cur_s"] = sb("cur_s", [P, NG * 16], F32)
            h["ex_s"] = sb("ex_s", [P, NG * 16], F32)
            h["msk_s"] = sb("msk_s", [P, NG * 16], F32)
            h["neg1_s"] = sb("neg1_s", [P, NG], F32)
            h["mk_s"] = sb("mk_s", [P, NG * 8], F32)
            h["ssum_s"] = sb("ssum_s", [P, NG], F32)
            h["comb_s"] = sb("comb_s", [16, TT], BF16)
            h["fa_sl"] = [sb("fa0", [P, TT], F32)]
            h["fy_sl"] = [sb("fy0", [P, TT], F32)]
            h["fx_sl"] = [sb("fx0", [P, TT], F32)]
            h["a_s"] = h["alog_s"]
            h["zsum"] = h["h_all"][:, 0:NC * NS]
            h["banks"] = [st_ctx.enter_context(
                nc.psum_tensor(f"bank{i}", [P, 512], F32)) for i in range(8)]

            semnames = (["pe", "dve", "act", "pool", "sp",
                         "x0", "x1", "xst0", "xst1", "xnld",
                         "yw0", "accst", "outst0", "const",
                         "fa0", "fy0", "fx0"] +
                        [f"wgu{i}" for i in range(4)] +
                        [f"wd{i}" for i in range(NWD)])
            for n in semnames:
                self.mksem(n, st_ctx)

            self.bk = [self.trk() for _ in range(8)]
            for nm in ["t_xnf", "t_acc", "t_h", "t_rows", "t_arep", "t_xn_t",
                       "t_dtr", "t_bc", "t_ball", "t_call", "t_u", "t_hall",
                       "t_silu", "t_upsc", "t_cw", "t_grep", "t_logit",
                       "t_ltm", "t_comb"]:
                setattr(self, nm, self.trk())
            for nm in ["t_x", "t_xbf", "t_xsq", "t_delta", "t_dA", "t_dBu",
                       "t_y", "t_fa", "t_fy", "t_fx"]:
                setattr(self, nm, [self.trk() for _ in range(2)])
            self.t_wgu = [self.trk() for _ in range(4)]
            self.t_wd = [self.trk() for _ in range(NWD)]
            self.t_sfp = [self.trk() for _ in range(3)]
            self.xst_done = [None] * NT
            self.yw_done = [None] * NT
            self.accst_done = [None] * NT
            self.wgu_rr = 0
            self.wd_rr = 0
            self.mm_rr = 0
            self.h_rvs = {}
            self.acc_dep = []
            self.summ_done = []
            self.sacc_frees = []

            self.emit_init()
            if cfg["OVERLAP"]:
                self.emit_attn_tile("xn", None, 0)
                y_walked = set()
                ffn_walked = set()
                fin_done = set()

                def attn_stream():
                    for t in range(1, NT):
                        yield from self.gen_attn_tile("xn", None, t)
                    for r in range(NPREV):
                        for t in range(NT):
                            yield from self.gen_attn_tile("carry", r, t)
                    self.emit_hin()
                    yield
                    for t in range(NT):
                        yield from self.gen_attn_tile("y", None, t)
                        y_walked.add(t)
                        yield

                def ffn_stream():
                    for s in range(NT):
                        yield from self.gen_ffn_st(s)
                        ffn_walked.add(s)
                        yield

                ag, fg = attn_stream(), ffn_stream()
                a_more = f_more = True
                i = 0
                while a_more or f_more:
                    i += 1
                    if f_more:
                        for _ in range(2 + (i % 2)):
                            try:
                                next(fg)
                            except StopIteration:
                                f_more = False
                                break
                    if a_more:
                        try:
                            next(ag)
                        except StopIteration:
                            a_more = False
                    for t in range(NT):
                        if (t in y_walked and t in ffn_walked
                                and t not in fin_done):
                            self.emit_fin_st(t)
                            fin_done.add(t)
                for t in range(NT):
                    if t not in fin_done:
                        self.emit_fin_st(t)
                        fin_done.add(t)
            else:
                for t in range(NT):
                    self.emit_attn_tile("xn", None, t)
                for r in range(NPREV):
                    for t in range(NT):
                        self.emit_attn_tile("carry", r, t)
                self.emit_hin()
                for t in range(NT):
                    self.emit_attn_tile("y", None, t)
                for s in range(NT):
                    for _ in self.gen_ffn_st(s):
                        pass
                for s in range(NT):
                    self.emit_fin_st(s)
            self.emit_end()

            with nc.Block() as block:
                @block.sync
                def _(e):
                    for f in self.q["sync"]:
                        f(e)

                @block.scalar
                def _(e):
                    for f in self.q["scalar"]:
                        f(e)

                @block.vector
                def _(e):
                    for f in self.q["vector"]:
                        f(e)

                @block.tensor
                def _(e):
                    for f in self.q["tensor"]:
                        f(e)

                @block.gpsimd
                def _(e):
                    for f in self.q["gpsimd"]:
                        f(e)
        return nc

    def emit_init(self):
        h, d = self.h, self.d
        loads = [
            ("wxp_s", "wxp", "(c p) k -> p c k"),
            ("wdt_s", "wdt", "(k p) dd -> p k dd"),
            ("gate_s", "gate_l", None),
            ("sgv_s", "sgv_l", None),
            ("alog_s", "alog", None), ("bdt_s", "bdt", None),
            ("d_s", "dpar", None), ("e16_s", "e16", None),
            ("id_s", "ident", None), ("onesb_s", "onesb", None),
            ("onesr_s", "onesr", None), ("sel_s", "selin", None),
        ]
        for sbn, dn, pat in loads:
            src = d[dn] if pat is None else d[dn].rearrange(pat, p=P)
            dst = h[sbn][:, :]
            if pat is not None:
                nk = src.shape[1]
                dst = dst.rearrange("p (a b) -> p a b", a=nk)
            self.dma("gpsimd", dst, src, "const")
        tot = self.cnt["const"]
        self.const_done = [("const", tot)]
        self.eop("scalar", lambda e: e.activation(
            out=h["alog_s"][:, :], in_=h["alog_s"][:, :], func=AF.Exp),
            waits=self.const_done)
        self.a_ready = self.eop("vector", lambda e: e.tensor_scalar_mul(
            h["a_s"][:, :], h["a_s"][:, :], -1.0),
            waits=self.now("act"))
        self.carry_last = self.eop("vector", lambda e: e.memset(
            h["carry"][:, :], 0.0))
        for j, v in [(0, 1e-5), (1, -1.0), (2, 1.0)]:
            self.cv_ready = self.eop("vector", lambda e, j=j, v=v: e.memset(
                h["cvals"][:, j:j + 1], v))

    def emit_attn_tile(self, kind, rng, t):
        for _ in self.gen_attn_tile(kind, rng, t):
            pass

    def gen_attn_tile(self, kind, rng, t):
        h, d, cfg = self.h, self.d, self.cfg
        D, TT, NS = cfg["D"], cfg["TT"], cfg["NS"]
        NC, NKD = D // P, cfg["DTR"] // P
        XK = cfg["DTR"] + 2 * NS
        NT = cfg["TOK"] // TT
        do_scan = kind in ("carry", "y")
        do_y = kind == "y"
        if kind == "carry":
            src, col0 = d["x_prev"], rng * cfg["TOK"] + t * TT
        else:
            src, col0 = d["x_own"], t * TT

        if kind == "carry" and t == 0:
            self.sacc_last = self.eop("gpsimd", lambda e: e.memset(
                h["sacc"][:, :], 0.0), waits=self.sacc_frees)
            self.sacc_frees = []

        bsum, bsq = self.bk[5], self.bk[6]
        pe_rv = None

        def stats_cast(c):
            sl = c % 2
            xs, tx = h["x_sl"][sl], self.t_x[sl]
            rv = self.dma("gpsimd", xs[:, :],
                          src[c * P:(c + 1) * P, col0:col0 + TT],
                          f"x{sl}", waits=tx["frees"] + self.const_done)
            self.produce(tx, rv)
            xbf, txb = h["xbf_sl"][sl], self.t_xbf[sl]
            rb = self.eop("gpsimd", lambda e, o=xbf, i=xs: e.tensor_copy(
                out=o[:, :], in_=i[:, :]), waits=tx["ready"] + txb["frees"])
            self.produce(txb, rb)
            self.consume(tx, rb)
            xsq, tsq = h["xsq_sl"][sl], self.t_xsq[sl]
            rq = self.eop("gpsimd", lambda e, o=xsq, i=xs: e.tensor_mul(
                o[:, :], i[:, :], i[:, :]),
                waits=tsq["frees"])
            self.produce(tsq, rq)
            self.consume(tx, rq)

        def stats_mm(c):
            nonlocal pe_rv
            sl = c % 2
            txb, tsq = self.t_xbf[sl], self.t_xsq[sl]
            w_extra = (bsum["frees"] + bsq["frees"]) if c == 0 else []
            self.eop("tensor", lambda e, c_=c, sl=sl: e.matmul(
                h["banks"][5][0:1, 0:TT], h["onesb_s"][:, :],
                h["xbf_sl"][sl][:, :],
                start=(c_ == 0), stop=(c_ == NC - 1)),
                waits=txb["ready"] + w_extra, inc=False)
            pe_rv = self.eop("tensor", lambda e, c_=c, sl=sl: e.matmul(
                h["banks"][6][0:1, 0:TT], h["onesb_s"][:, :],
                h["xsq_sl"][sl][:, :],
                start=(c_ == 0), stop=(c_ == NC - 1)),
                waits=tsq["ready"], inc=True)
            self.consume(txb, pe_rv)
            self.consume(tsq, pe_rv)

        stats_cast(0)
        stats_cast(1)
        yield
        for cb in range(0, NC, 2):
            if cb + 2 < NC:
                stats_cast(cb + 2)
                stats_cast(cb + 3)
            stats_mm(cb)
            stats_mm(cb + 1)
            yield
        self.produce(bsum, pe_rv)
        self.produce(bsq, pe_rv)
        R = h["rows"]
        tr = self.t_rows
        rv = self.eop("vector", lambda e: e.tensor_scalar_mul(
            R["m"][:, :], h["banks"][5][0:1, 0:TT], 1.0 / D),
            waits=bsum["ready"] + tr["frees"])
        self.consume(bsum, rv)
        rv = self.eop("vector", lambda e: e.tensor_scalar_mul(
            R["var"][:, :], h["banks"][6][0:1, 0:TT], 1.0 / D),
            waits=bsq["ready"])
        self.consume(bsq, rv)
        self.eop("vector", lambda e: e.tensor_mul(
            R["tmp"], R["m"], R["m"]))
        self.eop("vector", lambda e: e.tensor_sub(
            R["var"], R["var"], R["tmp"]))
        rv = self.eop("scalar", lambda e: e.activation(
            out=R["tmp"], in_=R["var"], func=AF.Sqrt,
            bias=h["cvals"][0:1, 0:1]), waits=self.now("dve"))
        rv = self.eop("vector", lambda e: e.reciprocal(
            R["var"], R["tmp"]), waits=[rv])
        rv = self.eop("vector", lambda e: e.scalar_tensor_tensor(
            out=R["tmp"], in0=R["m"], scalar=-1.0,
            in1=R["var"], op0=ALU.mult, op1=ALU.mult))
        self.produce(tr, rv)
        b5, b6 = self.bk[5], self.bk[6]
        self.eop("tensor", lambda e: e.matmul(
            h["banks"][5][:, 0:TT], h["onesr_s"][:, :], R["var"],
            start=True, stop=True), waits=[rv] + b5["frees"], inc=False)
        pe_rv = self.eop("tensor", lambda e: e.matmul(
            h["banks"][6][:, 0:TT], h["onesr_s"][:, :], R["tmp"],
            start=True, stop=True), waits=b6["frees"], inc=True)
        self.produce(b5, pe_rv)
        self.produce(b6, pe_rv)
        ta = self.t_arep
        rv1 = self.eop("scalar", lambda e: e.copy(
            out=h["arep_s"][:, :], in_=h["banks"][5][:, 0:TT]),
            waits=b5["ready"] + ta["frees"])
        self.consume(b5, rv1)
        rv2 = self.eop("scalar", lambda e: e.copy(
            out=h["brep_s"][:, :], in_=h["banks"][6][:, 0:TT]),
            waits=b6["ready"])
        self.consume(b6, rv2)
        self.consume(tr, rv2)
        ta["ready"] = [rv2]
        ta["frees"] = []
        yield

        txn = self.t_xn_t
        xdbl = []
        m0 = 0
        bi = 5
        while m0 < XK and do_scan:
            xdbl.append((bi, m0, min(P, XK - m0)))
            m0 += P
            bi += 1
        for c in range(NC):
            sl = c % 2
            xs, tx = h["x_sl"][sl], self.t_x[sl]
            rv = self.dma("gpsimd", xs[:, :],
                          src[c * P:(c + 1) * P, col0:col0 + TT],
                          f"x{sl}", waits=tx["frees"])
            self.produce(tx, rv)
            tA = self.t_sfp[0]
            self.eop("gpsimd", lambda e, i=xs: e.tensor_mul(
                h["sfpA"][:, :], i[:, :], h["arep_s"][:, :]),
                waits=tx["ready"] + ta["ready"] + tA["frees"])
            rv = self.eop("gpsimd", lambda e: e.tensor_add(
                h["sfpA"][:, :], h["sfpA"][:, :], h["brep_s"][:, :]))
            self.consume(tx, rv)
            self.produce(tA, rv)
            rvc = self.eop("gpsimd", lambda e, c_=c: e.tensor_copy(
                out=h["xn_t"][:, c_ * TT:(c_ + 1) * TT], in_=h["sfpA"][:, :]),
                waits=tA["ready"] + (txn["frees"] if c == 0 else []))
            self.consume(tA, rvc)
            if c == 0:
                txn["frees"] = []
            for bi_, mb, mlen in xdbl:
                w_extra = (self.bk[bi_]["frees"] if c == 0 else [])
                rvp = self.eop("tensor", lambda e, c_=c, bi_=bi_, mb=mb,
                               mlen=mlen: e.matmul(
                    h["banks"][bi_][0:mlen, 0:TT],
                    h["wxp_s"][:, c_ * XK + mb:c_ * XK + mb + mlen],
                    h["xn_t"][:, c_ * TT:(c_ + 1) * TT],
                    start=(c_ == 0), stop=(c_ == NC - 1)),
                    waits=[rvc] + w_extra + self.const_done,
                    inc=(c == NC - 1 and bi_ == xdbl[-1][0]))
            if do_scan and c == NC - 1:
                for bi_, _, _ in xdbl:
                    self.produce(self.bk[bi_], rvp)
            if c % 2 == 1 and c != NC - 1:
                yield
        txn["ready"] = [("pool", self.cnt["pool"])]

        if kind == "xn":
            sl = t % 2
            rv = self.dma("gpsimd", d["xn_dram"][t], h["xn_t"][:, :],
                          f"xst{sl}", waits=txn["ready"])
            self.xst_done[t] = rv
            txn["frees"] = [rv, ("pool", self.cnt["pool"])]
            return

        tdtr, tbc = self.t_dtr, self.t_bc
        for ki in range(NKD):
            bi_ = 5 + ki
            rv = self.eop("scalar", lambda e, ki=ki, bi_=bi_: e.copy(
                out=h["dtr_s"][:, ki * TT:(ki + 1) * TT],
                in_=h["banks"][bi_][:, 0:TT]),
                waits=self.bk[bi_]["ready"] + (tdtr["frees"] if ki == 0 else []))
            self.consume(self.bk[bi_], rv)
        tdtr["ready"] = [rv]
        tdtr["frees"] = []
        bcb = 5 + NKD
        nbc = 2 * NS
        rv = self.eop("scalar", lambda e, bcb=bcb: e.copy(
            out=h["bc_s"][:, :], in_=h["banks"][bcb][0:nbc, 0:TT]),
            waits=self.bk[bcb]["ready"] + tbc["frees"])
        self.consume(self.bk[bcb], rv)
        self.produce(tbc, rv)

        sel_jobs = [(0, "ball_s", self.t_ball)]
        if do_y:
            sel_jobs.append((1, "call_s", self.t_call))
        for which, tgt, ttrk in sel_jobs:
            for n in range(NS):
                bi_ = 7 if (n % 2 == 0) else 5
                j = n + NS * which
                rvp = self.eop("tensor", lambda e, j=j, bi_=bi_: e.matmul(
                    h["banks"][bi_][:, 0:TT], h["e16_s"][:, j * P:(j + 1) * P],
                    h["bc_s"][:, :], start=True, stop=True),
                    waits=tbc["ready"] + self.bk[bi_]["frees"], inc=True)
                self.produce(self.bk[bi_], rvp)
                rv = self.eop("scalar", lambda e, n=n, tgt=tgt, bi_=bi_: e.copy(
                    out=h[tgt][:, n * TT:(n + 1) * TT],
                    in_=h["banks"][bi_][:, 0:TT]),
                    waits=self.bk[bi_]["ready"] + (ttrk["frees"] if n == 0 else []))
                self.consume(self.bk[bi_], rv)
            ttrk["ready"] = [rv]
            ttrk["frees"] = []
            yield
        self.consume(tbc, rv)

        for c in range(NC):
            dbank = 5 if (c % 2 == 0) else 6
            dsl = c % 2
            dsb, tds = h["delta_sl"][dsl], self.t_delta[dsl]
            for ki in range(NKD):
                rvp = self.eop("tensor", lambda e, c_=c, ki=ki, bi_=dbank:
                               e.matmul(
                    h["banks"][bi_][:, 0:TT],
                    h["wdt_s"][:, ki * D + c_ * P:ki * D + (c_ + 1) * P],
                    h["dtr_s"][:, ki * TT:(ki + 1) * TT],
                    start=(ki == 0), stop=(ki == NKD - 1)),
                    waits=tdtr["ready"] +
                    (self.bk[dbank]["frees"] if ki == 0 else []),
                    inc=(ki == NKD - 1))
            self.produce(self.bk[dbank], rvp)
            tA, tB, tR = self.t_sfp
            bcol = h["bdt_s"][:, c:c + 1]
            rv1 = self.eop("scalar", lambda e, bi_=dbank, bcol=bcol:
                           e.activation(
                out=h["sfpA"][:, :], in_=h["banks"][bi_][:, 0:TT], func=AF.Abs,
                bias=bcol), waits=self.bk[dbank]["ready"] + tA["frees"])
            self.produce(tA, rv1)
            rv2 = self.eop("scalar", lambda e: e.activation(
                out=h["sfpB"][:, :], in_=h["sfpA"][:, :], func=AF.Exp,
                scale=h["cvals"][:, 1:2]), waits=tB["frees"] + tR["frees"])
            self.consume(tA, rv2)
            self.produce(tB, rv2)
            rv3 = self.eop("scalar", lambda e: e.activation(
                out=h["sfpA"][:, :], in_=h["sfpB"][:, :], func=AF.Ln,
                bias=h["cvals"][:, 2:3]))
            self.consume(tB, rv3)
            self.produce(tA, rv3)
            rv4 = self.eop("vector", lambda e, bi_=dbank, bcol=bcol:
                           e.tensor_scalar(
                out=h["sfpB"][:, :], in0=h["banks"][bi_][:, 0:TT],
                scalar1=bcol, scalar2=0.0, op0=ALU.add, op1=ALU.max),
                waits=self.bk[dbank]["ready"] + tR["frees"] + [rv3])
            self.consume(self.bk[dbank], rv4)
            self.produce(tR, rv4)
            if kind == "carry":
                rv5 = self.eop("vector", lambda e, dsb=dsb:
                               e.scalar_tensor_tensor(
                    out=dsb[:, :], in0=h["sfpB"][:, :], scalar=1.0,
                    in1=h["sfpA"][:, :], op0=ALU.mult, op1=ALU.add,
                    accum_out=h["spart"][:, :]),
                    waits=[rv3] + tds["frees"])
                self.sacc_last = self.eop("gpsimd", lambda e, c_=c:
                                          e.tensor_add(
                    h["sacc"][:, c_:c_ + 1], h["sacc"][:, c_:c_ + 1],
                    h["spart"][:, :]), waits=[rv5])
            else:
                rv5 = self.eop("gpsimd", lambda e, dsb=dsb: e.tensor_add(
                    dsb[:, :], h["sfpB"][:, :], h["sfpA"][:, :]),
                    waits=[rv3, rv4] + tds["frees"])
            self.consume(tA, rv5)
            self.consume(tR, rv5)
            self.produce(tds, rv5)
            tu = self.t_u
            rvu = self.eop("gpsimd", lambda e, c_=c, dsb=dsb: e.tensor_mul(
                h["u_s"][:, :], dsb[:, :], h["xn_t"][:, c_ * TT:(c_ + 1) * TT]),
                waits=tds["ready"] + tu["frees"] + txn["ready"])
            self.produce(tu, rvu)
            self.consume(tds, rvu)
            for n in range(NS):
                asl = n % 2
                dAb, tdA = h["dA_sl"][asl], self.t_dA[asl]
                dBb, tdB = h["dBu_sl"][asl], self.t_dBu[asl]
                acol = h["a_s"][:, c * NS + n:c * NS + n + 1]
                rva = self.eop("scalar", lambda e, dsb=dsb, dAb=dAb, acol=acol:
                               e.activation(out=dAb[:, :], in_=dsb[:, :],
                                            func=AF.Exp, scale=acol),
                               waits=tds["ready"] + tdA["frees"] +
                               [self.a_ready])
                self.produce(tdA, rva)
                rvb = self.eop("gpsimd", lambda e, n=n, dBb=dBb: e.tensor_mul(
                    dBb[:, :], h["u_s"][:, :],
                    h["ball_s"][:, n * TT:(n + 1) * TT]),
                    waits=tu["ready"] + tdB["frees"] + self.t_ball["ready"])
                self.produce(tdB, rvb)
                ic = h["carry"][:, c * NS + n:c * NS + n + 1]
                dst = h["h_all"][:, n * TT:(n + 1) * TT]
                rvs = self.eop("vector", lambda e, dAb=dAb, dBb=dBb, ic=ic,
                               dst=dst: e.tensor_tensor_scan(
                    out=dst, data0=dAb[:, :], data1=dBb[:, :], initial=ic,
                    op0=ALU.mult, op1=ALU.add),
                    waits=tdA["ready"] + tdB["ready"] +
                    ([self.carry_last] if (n == 0 and c == 0) else []) +
                    (self.t_hall["frees"] if (n == 0 and c == 0) else []))
                self.consume(tdA, rvs)
                self.consume(tdB, rvs)
                if n == NS - 1:
                    self.consume(tu, rvs)
            self.carry_last = self.eop("gpsimd", lambda e, c_=c: e.tensor_copy(
                out=h["carry"][:, c_ * NS:(c_ + 1) * NS],
                in_=h["h_all"][:, :].rearrange("p (n l) -> p n l", n=NS)
                [:, :, TT - 1]), waits=self.now("dve"))
            if do_y:
                self.eop("gpsimd", lambda e: e.tensor_mul(
                    h["h_all"][:, :], h["h_all"][:, :], h["call_s"][:, :]),
                    waits=self.t_call["ready"] + self.now("dve"))
                half = NS * TT // 2
                while half >= TT:
                    self.eop("gpsimd", lambda e, hf=half: e.tensor_add(
                        h["h_all"][:, 0:hf], h["h_all"][:, 0:hf],
                        h["h_all"][:, hf:2 * hf]))
                    half //= 2
                ysl = 0
                yb, ty = h["y_sl"][0], self.t_y[0]
                dcol = h["d_s"][:, c:c + 1]
                self.eop("gpsimd", lambda e, c_=c, yb=yb, dcol=dcol:
                         e.tensor_scalar(
                    out=yb[:, :], in0=h["xn_t"][:, c_ * TT:(c_ + 1) * TT],
                    scalar1=dcol, scalar2=None, op0=ALU.mult),
                    waits=ty["frees"] + txn["ready"])
                rvy = self.eop("gpsimd", lambda e, yb=yb: e.tensor_add(
                    yb[:, :], yb[:, :], h["h_all"][:, 0:TT]))
                self.produce(ty, rvy)
                yield
                rvd = self.dma("gpsimd",
                               d["y_dram"][c * P:(c + 1) * P, col0:col0 + TT],
                               yb[:, :], "yw0", waits=ty["ready"])
                ty["frees"].append(rvd)
            else:
                yield
        if do_y:
            self.yw_done[t] = [("yw0", self.cnt["yw0"])]
        endm = self.now("dve", "pe", "act", "pool")
        for tt_ in [txn, self.t_ball, self.t_call, ta, tdtr]:
            tt_["frees"] = list(endm)
        self.t_hall["frees"] = self.now("dve", "pool")

        if kind == "carry" and t == NT - 1:
            r = rng
            self.eop("gpsimd", lambda e, r=r: e.tensor_copy(
                out=h["br_s"][r][:, :], in_=h["carry"][:, :]))
            NCc = D // P
            rvz = self.eop("gpsimd", lambda e, NCc=NCc: e.tensor_tensor(
                out=h["zsum"].rearrange("p (c n) -> p c n", c=NCc),
                in0=h["sacc"][:, :].unsqueeze(2).broadcast_to([P, NCc, NS]),
                in1=h["a_s"][:, :].rearrange("p (c n) -> p c n", c=NCc),
                op=ALU.mult),
                waits=[self.sacc_last, self.a_ready] + self.now("dve"))
            self.sacc_frees = [rvz]
            rva = self.eop("scalar", lambda e, r=r: e.activation(
                out=h["ar_s"][r][:, :], in_=h["zsum"], func=AF.Exp),
                waits=[rvz])
            self.summ_done += [rva, rvz]
            self.t_hall["frees"].append(rva)
            if self.cfg.get("DEBUG"):
                self.dma("gpsimd", d[f"dbg_ar{r}"], h["ar_s"][r][:, :], "const",
                         waits=[rva])
                self.dma("gpsimd", d[f"dbg_br{r}"], h["br_s"][r][:, :], "const",
                         waits=self.now("dve"))
                if r == 0:
                    self.dma("gpsimd", d["dbg_sacc"], h["sacc"][:, :], "const",
                             waits=[self.sacc_last])
            self.carry_last = self.eop("gpsimd", lambda e: e.memset(
                h["carry"][:, :], 0.0))

    def emit_hin(self):
        h = self.h
        NPREV = self.cfg["NPREV"]
        rv = None
        for r in range(NPREV):
            sel = h["sel_s"][:, r:r + 1]
            isel = h["sel_s"][:, NPREV + r:NPREV + r + 1]
            self.eop("gpsimd", lambda e, r=r, sel=sel, isel=isel:
                     e.tensor_scalar(
                out=h["zsum"], in0=h["ar_s"][r][:, :], scalar1=sel,
                scalar2=isel, op0=ALU.mult, op1=ALU.add),
                waits=self.summ_done + [self.carry_last] + self.now("dve"))
            self.eop("gpsimd", lambda e: e.tensor_mul(
                h["carry"][:, :], h["carry"][:, :], h["zsum"]))
            self.eop("gpsimd", lambda e, r=r, sel=sel: e.tensor_scalar(
                out=h["zsum"], in0=h["br_s"][r][:, :], scalar1=sel,
                scalar2=None, op0=ALU.mult))
            rv = self.eop("gpsimd", lambda e: e.tensor_add(
                h["carry"][:, :], h["carry"][:, :], h["zsum"]))
        self.carry_last = rv
        if self.cfg.get("DEBUG"):
            self.dma("gpsimd", self.d["dbg_hin"], self.h["carry"][:, :], "const",
                     waits=[rv])
            self.wait("gpsimd", "const", self.cnt["const"])

    def gen_ffn_st(self, s):
        h, d, cfg = self.h, self.d, self.cfg
        D, TT, E, I, SI = cfg["D"], cfg["TT"], cfg["E"], cfg["I"], cfg["SI"]
        NC, NU, NQ = D // P, I // P, SI // I
        NG = TT // P
        col0 = s * TT

        txnf = self.t_xnf
        rv = self.dma("scalar", h["xnf_s"][:, :], d["xn_dram"][s],
                      "xnld", waits=txnf["frees"] + [self.xst_done[s]])
        self.produce(txnf, rv)

        def xn_chunk(c):
            return h["xnf_s"][:, c * TT:(c + 1) * TT]

        tacc = self.t_acc
        acc_w = list(tacc["frees"])

        b4 = self.bk[4]
        for c in range(NC):
            rvp = self.eop("tensor", lambda e, c_=c: e.matmul(
                h["banks"][4][0:16, 0:TT], h["gate_s"][:, c_ * E:(c_ + 1) * E],
                xn_chunk(c_), start=(c_ == 0), stop=(c_ == NC - 1)),
                waits=txnf["ready"] + (b4["frees"] if c == 0 else []),
                inc=(c == NC - 1))
        self.produce(b4, rvp)
        tl = self.t_logit
        rv = self.eop("scalar", lambda e: e.copy(
            out=h["logit_s"][:, :], in_=h["banks"][4][0:16, 0:TT]),
            waits=b4["ready"] + tl["frees"])
        self.consume(b4, rv)
        self.produce(tl, rv)
        tltm = self.t_ltm
        for g in range(NG):
            rvp = self.eop("tensor", lambda e, g=g: e.transpose(
                h["banks"][4][:, 0:16], h["logit_s"][:, g * P:(g + 1) * P],
                h["id_s"][0:16, 0:16]),
                waits=tl["ready"] + b4["frees"], inc=True)
            self.produce(b4, rvp)
            rv = self.eop("scalar", lambda e, g=g: e.copy(
                out=h["ltm_s"][:, g * 16:(g + 1) * 16],
                in_=h["banks"][4][:, 0:16]),
                waits=b4["ready"] + (tltm["frees"] if g == 0 else []))
            self.consume(b4, rv)
        self.consume(tl, rv)
        act_ltm = rv
        K4 = cfg["TOPK"]
        for g in range(NG):
            L = h["ltm_s"][:, g * 16:(g + 1) * 16]
            C = h["cur_s"][:, g * 16:(g + 1) * 16]
            t8 = h["mk_s"][:, g * 8:(g + 1) * 8]
            self.eop("vector", lambda e, L=L, t8=t8: e.max(
                out=t8, in_=L), waits=[act_ltm])
            n1 = h["neg1_s"][:, g:g + 1]
            self.eop("vector", lambda e, t8=t8, n1=n1: e.tensor_scalar_mul(
                n1, t8[:, 0:1], -1.0))
            rve = self.eop("scalar", lambda e, L=L, g=g, n1=n1: e.activation(
                out=h["ex_s"][:, g * 16:(g + 1) * 16], in_=L, func=AF.Exp,
                bias=n1), waits=self.now("dve"))
            self.eop("vector", lambda e, t8=t8: e.memset(
                t8[:, K4:8], 1e30))
            self.eop("vector", lambda e, L=L, C=C, t8=t8: e.match_replace(
                out=C, in_to_replace=t8, in_values=L, imm_value=NBIG))
            Mg = h["msk_s"][:, g * 16:(g + 1) * 16]
            self.eop("vector", lambda e, C=C, L=L, Mg=Mg: e.tensor_tensor(
                out=Mg, in0=C, in1=L, op=ALU.not_equal))
            self.eop("vector", lambda e, g=g, Mg=Mg: e.tensor_mul(
                Mg, Mg, h["ex_s"][:, g * 16:(g + 1) * 16]), waits=[rve])
            sm = h["ssum_s"][:, g:g + 1]
            self.eop("vector", lambda e, Mg=Mg, sm=sm: e.reduce_sum(
                out=sm, in_=Mg, axis=AX.X))
            self.eop("vector", lambda e, sm=sm: e.reciprocal(sm, sm))
            rv = self.eop("vector", lambda e, Mg=Mg, sm=sm: e.tensor_scalar_mul(
                Mg, Mg, sm))
        tc = self.t_comb
        for g in range(NG):
            rvp = self.eop("tensor", lambda e, g=g: e.transpose(
                h["banks"][4][0:16, 0:P], h["msk_s"][:, g * 16:(g + 1) * 16],
                h["id_s"][:, :]),
                waits=[rv] + b4["frees"], inc=True)
            self.produce(b4, rvp)
            rvc = self.eop("scalar", lambda e, g=g: e.copy(
                out=h["comb_s"][:, g * P:(g + 1) * P],
                in_=h["banks"][4][0:16, 0:P]),
                waits=b4["ready"] + (tc["frees"] if g == 0 else []))
            self.consume(b4, rvc)
        tc["ready"] = [rvc]
        tc["frees"] = []

        for c in range(NC):
            rvp = self.eop("tensor", lambda e, c_=c: e.matmul(
                h["banks"][4][0:1, 0:TT], h["sgv_s"][:, c_:c_ + 1],
                xn_chunk(c_), start=(c_ == 0), stop=(c_ == NC - 1)),
                waits=(b4["frees"] if c == 0 else []), inc=(c == NC - 1))
        self.produce(b4, rvp)
        rv = self.eop("scalar", lambda e: e.activation(
            out=h["logit_s"][0:1, 0:TT], in_=h["banks"][4][0:1, 0:TT],
            func=AF.Sigmoid), waits=b4["ready"])
        self.consume(b4, rv)
        rvp = self.eop("tensor", lambda e: e.matmul(
            h["banks"][4][:, 0:TT], h["onesr_s"][:, :], h["logit_s"][0:1, 0:TT],
            start=True, stop=True), waits=[rv] + b4["frees"], inc=True)
        self.produce(b4, rvp)
        tg = self.t_grep
        rv = self.eop("scalar", lambda e: e.copy(
            out=h["grep_s"][:, :], in_=h["banks"][4][:, 0:TT]),
            waits=b4["ready"] + tg["frees"])
        self.consume(b4, rv)
        self.produce(tg, rv)
        yield

        ROT = [0, 1, 2, 3, 4]

        def nextbank():
            bi = ROT[self.mm_rr % len(ROT)]
            self.mm_rr += 1
            return bi

        NCH = NC // 2

        def gu_unit(src_dram, unit):
            bks = []
            for half in range(2):
                bi = nextbank()
                bks.append(bi)
                for kh in range(2):
                    wsl = self.wgu_rr % 4
                    self.wgu_rr += 1
                    wt = self.t_wgu[wsl]
                    rv = self.dma("sync", h["wgu_sl"][wsl][:, :],
                                  src_dram[2 * unit + half]
                                  [:, kh * NCH * P:(kh + 1) * NCH * P],
                                  f"wgu{wsl}", waits=wt["frees"])
                    self.produce(wt, rv)
                    for cl in range(NCH):
                        c = kh * NCH + cl
                        rvp = self.eop("tensor", lambda e, c_=c, cl=cl, bi=bi,
                                       wsl=wsl: e.matmul(
                            h["banks"][bi][:, 0:TT],
                            h["wgu_sl"][wsl][:, cl * P:(cl + 1) * P],
                            xn_chunk(c_),
                            start=(c_ == 0), stop=(c_ == NC - 1)),
                            waits=wt["ready"] +
                            (self.bk[bi]["frees"] if c == 0 else []),
                            inc=(cl == NCH - 1))
                    self.consume(wt, rvp)
                self.produce(self.bk[bi], rvp)
            return bks

        def unit_epilogue(j_local, use_cw, bg, bu):
            ts_ = self.t_silu
            rv1 = self.eop("scalar", lambda e, bg=bg: e.activation(
                out=h["silu_s"][:, :], in_=h["banks"][bg][:, 0:TT],
                func=AF.Silu), waits=self.bk[bg]["ready"] + ts_["frees"])
            self.consume(self.bk[bg], rv1)
            self.produce(ts_, rv1)
            tu_ = self.t_upsc
            if use_cw:
                rv2 = self.eop("vector", lambda e, bu=bu:
                               e.scalar_tensor_tensor(
                    out=h["upsc_s"][:, :], in0=h["banks"][bu][:, 0:TT],
                    scalar=1.0, in1=h["cw_s"][:, :], op0=ALU.mult,
                    op1=ALU.mult),
                    waits=self.bk[bu]["ready"] + tu_["frees"] +
                    self.t_cw["ready"])
            else:
                rv2 = self.eop("vector", lambda e, bu=bu: e.tensor_copy(
                    out=h["upsc_s"][:, :], in_=h["banks"][bu][:, 0:TT]),
                    waits=self.bk[bu]["ready"] + tu_["frees"])
            self.consume(self.bk[bu], rv2)
            self.produce(tu_, rv2)
            rv3 = self.eop("vector", lambda e, j=j_local: e.tensor_mul(
                h["h_s"][:, j * TT:(j + 1) * TT], h["silu_s"][:, :],
                h["upsc_s"][:, :]),
                waits=[rv1] + (self.t_h["frees"] if j_local == 0 else []))
            self.consume(ts_, rv3)
            self.consume(tu_, rv3)
            self.h_rvs[j_local] = rv3
            return rv3

        def down_pass(src_dram, mode):
            h_rvs = dict(self.h_rvs)
            first = True
            for cg in range(NC // 2):
                wsl = self.wd_rr % len(h["wd_sl"])
                self.wd_rr += 1
                wt = self.t_wd[wsl]
                rv = self.dma("sync", h["wd_sl"][wsl][:, :],
                              src_dram[cg],
                              f"wd{wsl}", waits=wt["frees"])
                self.produce(wt, rv)
                for mi in range(2):
                    bi = nextbank()
                    c_out = cg * 2 + mi
                    for k in range(NU):
                        rvp = self.eop("tensor", lambda e, k=k, mi=mi, bi=bi,
                                       wsl=wsl: e.matmul(
                            h["banks"][bi][:, 0:TT],
                            h["wd_sl"][wsl][:, k * 2 * P + mi * P:
                                            k * 2 * P + (mi + 1) * P],
                            h["h_s"][:, k * TT:(k + 1) * TT],
                            start=(k == 0), stop=(k == NU - 1)),
                            waits=wt["ready"] + [h_rvs[k]] +
                            (self.bk[bi]["frees"] if k == 0 else []),
                            inc=(k == NU - 1))
                    self.produce(self.bk[bi], rvp)
                    if mi == 1:
                        self.consume(wt, rvp)
                    aw = (acc_w + self.acc_dep) if first else []
                    first = False
                    if mode == "init":
                        rva = self.eop("scalar", lambda e, c_=c_out, bi=bi:
                                       e.copy(
                            out=h["acc_s"][:, c_ * TT:(c_ + 1) * TT],
                            in_=h["banks"][bi][:, 0:TT]),
                            waits=self.bk[bi]["ready"] + aw)
                    else:
                        rva = self.eop("vector", lambda e, c_=c_out, bi=bi:
                                       e.tensor_add(
                            h["acc_s"][:, c_ * TT:(c_ + 1) * TT],
                            h["acc_s"][:, c_ * TT:(c_ + 1) * TT],
                            h["banks"][bi][:, 0:TT]),
                            waits=self.bk[bi]["ready"] + aw)
                    self.consume(self.bk[bi], rva)
                    self.last_down = rva
                if cg % 2 == 1:
                    yield

        for qq in range(NQ):
            rvh = None
            for j in range(NU):
                bg, bu = gu_unit(d["sgu"], qq * NU + j)
                rvh = unit_epilogue(j, use_cw=False, bg=bg, bu=bu)
                yield
            self.produce(self.t_h, rvh)
            yield from down_pass(
                d["sdn"][qq * (NC // 2):(qq + 1) * (NC // 2)],
                "init" if qq == 0 else "add")
            last = self.last_down
            self.t_h["frees"] = [last]
            self.acc_dep = [last]
        acc_sc = []
        for c in range(NC):
            rv = self.eop("vector", lambda e, c_=c: e.tensor_mul(
                h["acc_s"][:, c_ * TT:(c_ + 1) * TT],
                h["acc_s"][:, c_ * TT:(c_ + 1) * TT], h["grep_s"][:, :]),
                waits=tg["ready"] + self.acc_dep)
            acc_sc.append(rv)
        self.consume(tg, rv)
        self.acc_dep = [rv]

        for ei in range(E):
            rvp = self.eop("tensor", lambda e, ei=ei: e.matmul(
                h["banks"][4][:, 0:TT], h["e16_s"][:, ei * P:(ei + 1) * P],
                h["comb_s"][:, :], start=True, stop=True),
                waits=tc["ready"] + b4["frees"], inc=True)
            self.produce(b4, rvp)
            tcw = self.t_cw
            rv = self.eop("scalar", lambda e: e.copy(
                out=h["cw_s"][:, :], in_=h["banks"][4][:, 0:TT]),
                waits=b4["ready"] + tcw["frees"])
            self.consume(b4, rv)
            self.produce(tcw, rv)
            if ei == E - 1:
                self.consume(tc, rv)
            rvh = None
            for j in range(NU):
                bg, bu = gu_unit(d["wgu"][ei * 2 * NU:(ei + 1) * 2 * NU], j)
                rvh = unit_epilogue(j, use_cw=True, bg=bg, bu=bu)
                if j == NU - 1:
                    self.consume(tcw, rvh)
                yield
            self.produce(self.t_h, rvh)
            yield from down_pass(
                d["wdn"][ei * (NC // 2):(ei + 1) * (NC // 2)], "add")
            last = self.last_down
            self.t_h["frees"] = [last]
            self.acc_dep = [last]

        rv = self.dma("sync",
                      d["acc_dram"].rearrange("(c p) l -> p c l", p=P)
                      [:, :, col0:col0 + TT],
                      h["acc_s"][:, :].rearrange("p (c l) -> p c l", c=NC),
                      "accst", waits=self.now("dve") + acc_sc + self.acc_dep)
        self.accst_done[s] = rv
        tacc["frees"] = [rv]
        self.acc_dep = []
        txnf["frees"] = [("pe", self.cnt["pe"])]

    def emit_fin_st(self, s):
        h, d, cfg = self.h, self.d, self.cfg
        D, TT = cfg["D"], cfg["TT"]
        NC = D // P
        col0 = s * TT
        yws = self.yw_done[s]
        for c in range(NC):
            sl = 0
            ta, tyy, txx = self.t_fa[0], self.t_fy[0], self.t_fx[0]
            rva = self.dma("gpsimd", h["fa_sl"][0][:, :],
                           d["acc_dram"][c * P:(c + 1) * P, col0:col0 + TT],
                           "fa0", waits=ta["frees"] + [self.accst_done[s]])
            self.produce(ta, rva)
            rvy = self.dma("gpsimd", h["fy_sl"][0][:, :],
                           d["y_dram"][c * P:(c + 1) * P, col0:col0 + TT],
                           "fy0", waits=tyy["frees"] + yws)
            self.produce(tyy, rvy)
            rvx = self.dma("gpsimd", h["fx_sl"][0][:, :],
                           d["x_own"][c * P:(c + 1) * P, col0:col0 + TT],
                           "fx0", waits=txx["frees"])
            self.produce(txx, rvx)
            rv1 = self.eop("vector", lambda e, sl=sl: e.tensor_add(
                h["fa_sl"][sl][:, :], h["fa_sl"][sl][:, :],
                h["fy_sl"][0][:, :]),
                waits=ta["ready"] + tyy["ready"])
            self.consume(tyy, rv1)
            rv2 = self.eop("vector", lambda e, sl=sl: e.tensor_add(
                h["fa_sl"][sl][:, :], h["fa_sl"][sl][:, :],
                h["fx_sl"][0][:, :]), waits=txx["ready"])
            self.consume(txx, rv2)
            rvo = self.dma("gpsimd",
                           d["out_fm"][c * P:(c + 1) * P, col0:col0 + TT],
                           h["fa_sl"][0][:, :], "outst0", waits=[rv2])
            self.consume(ta, rvo)
            ta["frees"].append(rvo)

    def emit_end(self):
        self.wait("gpsimd", "outst0", self.cnt["outst0"])


def build_program(cfg):
    _check(cfg)
    nc = bass.Bass("TRN2", target_bir_lowering=False)
    Builder(nc, cfg).build()
    return nc


def _pair_cols(m, inter):
    nj = inter // P
    perm = np.empty(2 * inter, np.int64)
    for j in range(nj):
        perm[j * 2 * P:j * 2 * P + P] = np.arange(j * P, (j + 1) * P)
        perm[j * 2 * P + P:(j + 1) * 2 * P] = inter + np.arange(j * P, (j + 1) * P)
    return m[:, perm]


def _prep_shared(w, cfg):
    D, I, SI, E = cfg["D"], cfg["I"], cfg["SI"], cfg["E"]
    NS = cfg["NS"]
    NC = D // P
    out = {}
    out["wxp"] = np.ascontiguousarray(w["L0_attn_W_xproj"].T).astype(BF)
    out["wdt"] = np.ascontiguousarray(w["L0_attn_W_dt"].T).astype(BF)
    out["alog"] = np.ascontiguousarray(
        w["L0_attn_A_log"].reshape(NC, P, NS).transpose(1, 0, 2)
        .reshape(P, NC * NS)).astype(np.float32)
    out["bdt"] = np.ascontiguousarray(
        w["L0_attn_b_dt"].reshape(NC, P).T).astype(np.float32)
    out["dpar"] = np.ascontiguousarray(
        w["L0_attn_D_param"].reshape(NC, P).T).astype(np.float32)
    out["gate_l"] = np.ascontiguousarray(
        w["L0_ffn_gate_w"].T.reshape(NC, P, E).transpose(1, 0, 2)
        .reshape(P, NC * E)).astype(BF)
    NU = I // P
    NCG = D // (2 * P)
    NQ = SI // I
    gu = w["L0_ffn_expert_gate_up"]
    wgu = np.empty((E, 2 * NU, P, NC * P), BF)
    for e in range(E):
        pw = _pair_cols(gu[e].T, I)
        wgu[e] = pw.reshape(NC, P, 2 * NU, P).transpose(2, 1, 0, 3) \
            .reshape(2 * NU, P, NC * P).astype(BF)
    out["wgu"] = wgu.reshape(E * 2 * NU, P, NC * P)
    wdn = np.empty((E, NCG, P, NU * 2 * P), BF)
    for e in range(E):
        dw = w["L0_ffn_expert_down"][e].T
        wdn[e] = dw.reshape(NU, P, NCG, 2 * P).transpose(2, 1, 0, 3) \
            .reshape(NCG, P, NU * 2 * P).astype(BF)
    out["wdn"] = wdn.reshape(E * NCG, P, NU * 2 * P)
    ps = _pair_cols(np.ascontiguousarray(w["L0_ffn_shared_gate_up"].T), SI)
    out["sgu"] = np.ascontiguousarray(
        ps.reshape(NC, P, 2 * SI // P, P).transpose(2, 1, 0, 3)
        .reshape(2 * SI // P, P, NC * P)).astype(BF)
    sd = np.ascontiguousarray(w["L0_ffn_shared_down"].T)
    out["sdn"] = np.ascontiguousarray(
        sd.reshape(NQ, NU, P, NCG, 2 * P).transpose(0, 3, 2, 1, 4)
        .reshape(NQ * NCG, P, NU * 2 * P)).astype(BF)
    out["sgv_l"] = np.ascontiguousarray(
        w["L0_ffn_shared_gate_vec"].reshape(NC, P).T).astype(BF)
    e16 = np.zeros((16, 16 * P), np.float32)
    for j in range(16):
        e16[j, j * P:(j + 1) * P] = 1.0
    out["e16"] = e16.astype(BF)
    out["ident"] = np.eye(P, dtype=np.float32)
    out["onesb"] = np.ones((P, 1), BF)
    out["onesr"] = np.ones((1, P), np.float32)
    return out


def kernel(L0_attn_x, L0_attn_W_xproj, L0_attn_W_dt, L0_attn_b_dt,
           L0_attn_A_log, L0_attn_D_param, L0_ffn_gate_w,
           L0_ffn_expert_gate_up, L0_ffn_expert_down, L0_ffn_shared_gate_up,
           L0_ffn_shared_down, L0_ffn_shared_gate_vec):
    cfg = dict(CFG_FULL)
    TOK, NPREV = cfg["TOK"], cfg["NPREV"]
    x = np.asarray(L0_attn_x, np.float32)
    B, L, D = x.shape
    ncore = 8
    nq = ncore // B
    assert L == nq * TOK and D == cfg["D"]
    w = {k: np.asarray(v, np.float32) for k, v in dict(
        L0_attn_W_xproj=L0_attn_W_xproj, L0_attn_W_dt=L0_attn_W_dt,
        L0_attn_b_dt=L0_attn_b_dt, L0_attn_A_log=L0_attn_A_log,
        L0_attn_D_param=L0_attn_D_param, L0_ffn_gate_w=L0_ffn_gate_w,
        L0_ffn_expert_gate_up=L0_ffn_expert_gate_up,
        L0_ffn_expert_down=L0_ffn_expert_down,
        L0_ffn_shared_gate_up=L0_ffn_shared_gate_up,
        L0_ffn_shared_down=L0_ffn_shared_down,
        L0_ffn_shared_gate_vec=L0_ffn_shared_gate_vec).items()}
    shared = _prep_shared(w, cfg)

    nc = build_program(cfg)
    in_maps = []
    for core in range(ncore):
        b, q = core // nq, core % nq
        xo = np.ascontiguousarray(x[b, q * TOK:(q + 1) * TOK, :].T)
        xp = np.empty((D, NPREV * TOK), np.float32)
        for r in range(NPREV):
            sq = max(q - NPREV + r, 0)
            xp[:, r * TOK:(r + 1) * TOK] = x[b, sq * TOK:(sq + 1) * TOK, :].T
        sel = np.zeros((P, 2 * NPREV), np.float32)
        for r in range(NPREV):
            active = (q - NPREV + r) >= 0
            sel[:, r] = 1.0 if active else 0.0
            sel[:, NPREV + r] = 0.0 if active else 1.0
        m = dict(shared)
        m["x_own"] = xo
        m["x_prev"] = np.ascontiguousarray(xp)
        m["selin"] = sel
        in_maps.append(m)

    res = run_bass_kernel_spmd(nc, in_maps, core_ids=list(range(ncore)))
    out = np.empty((B, L, D), np.float32)
    for core in range(ncore):
        b, q = core // nq, core % nq
        out[b, q * TOK:(q + 1) * TOK, :] = res.results[core]["out_fm"].T
    return out


# version 12
# speedup vs baseline: 1.0622x; 1.0622x over previous
import contextlib
import numpy as np
import ml_dtypes
import concourse.bass as bass
import concourse.mybir as mybir
from concourse.bass_utils import run_bass_kernel_spmd

F32 = mybir.dt.float32
BF16 = mybir.dt.bfloat16
AF = mybir.ActivationFunctionType
ALU = mybir.AluOpType
AX = mybir.AxisListType
BF = ml_dtypes.bfloat16

P = 128
NBIG = -1e30

CFG_FULL = dict(
    D=2048, TOK=4096, TT=512, DTR=256, NS=8, E=16, I=1408, SI=5632,
    NPREV=3, TOPK=4, OVERLAP=True,
    NGU_RING=2, NWD_RING=2, SIM_SYNC=True,
)


def _check(cfg):
    assert cfg["D"] % P == 0 and cfg["TOK"] % cfg["TT"] == 0
    assert cfg["DTR"] % P == 0 and cfg["I"] % P == 0 and cfg["SI"] % cfg["I"] == 0
    assert cfg["TT"] % P == 0 and cfg["NS"] == 8 and cfg["TT"] <= 512


class Builder:
    ENG_SEM = {"tensor": "pe", "vector": "dve", "scalar": "act",
               "gpsimd": "pool", "sync": "sp"}

    def __init__(self, nc, cfg):
        self.nc = nc
        self.cfg = cfg
        self.same_sync = cfg.get("SIM_SYNC", False)
        self.q = {e: [] for e in self.ENG_SEM}
        self.cnt = {}
        self.sems = {}
        self.lastwait = {e: {} for e in self.ENG_SEM}

    def emit(self, eng, fn):
        self.q[eng].append(fn)

    def mksem(self, name, stack):
        h = stack.enter_context(self.nc.semaphore(name))
        self.sems[name] = h
        self.cnt[name] = 0

    def wait(self, eng, s, v):
        if v <= 0 or (self.ENG_SEM[eng] == s and not self.same_sync):
            return
        lw = self.lastwait[eng]
        if lw.get(s, 0) >= v:
            return
        lw[s] = v
        h = self.sems[s]
        self.emit(eng, lambda e, h=h, v=v: e.wait_ge(h, v))

    def waits(self, eng, ws):
        best = {}
        for s, v in ws:
            if v > best.get(s, 0):
                best[s] = v
        for s, v in best.items():
            self.wait(eng, s, v)

    def eop(self, eng, build, waits=(), inc=True):
        s = self.ENG_SEM[eng]
        if self.same_sync:
            self.waits(eng, list(waits) + [(s, self.cnt[s])])
        else:
            self.waits(eng, waits)
        if inc:
            self.cnt[s] += 1
            v = self.cnt[s]
            h = self.sems[s]
            self.emit(eng, lambda e, b=build, h=h: b(e).then_inc(h, 1))
            return (s, v)
        self.emit(eng, lambda e, b=build: b(e))
        return (s, self.cnt[s])

    def dma(self, eng, out_ap, in_ap, slotsem, waits=()):
        self.waits(eng, waits)
        self.cnt[slotsem] += 16
        v = self.cnt[slotsem]
        h = self.sems[slotsem]
        self.emit(eng, lambda e, o=out_ap, i=in_ap, h=h:
                  e.dma_start(out=o, in_=i).then_inc(h, 16))
        return (slotsem, v)

    @staticmethod
    def trk():
        return {"ready": [], "frees": []}

    def produce(self, t, rv):
        t["ready"] = [rv]
        t["frees"] = []

    def consume(self, t, rv):
        t["frees"].append(rv)

    def now(self, *sems):
        return [(s, self.cnt[s]) for s in sems]

    def build(self):
        nc, cfg = self.nc, self.cfg
        D, TOK, TT = cfg["D"], cfg["TOK"], cfg["TT"]
        DTR, NS, E, I, SI = cfg["DTR"], cfg["NS"], cfg["E"], cfg["I"], cfg["SI"]
        NPREV = cfg["NPREV"]
        NC = D // P
        NT = TOK // TT
        NKD = DTR // P
        NU = I // P
        NQ = SI // I
        XK = DTR + 2 * NS
        NGU, NWD = cfg["NGU_RING"], cfg["NWD_RING"]

        dt = nc.dram_tensor
        self.d = d = {}
        d["x_own"] = dt("x_own", [D, TOK], F32, kind="ExternalInput").ap()
        d["x_prev"] = dt("x_prev", [D, NPREV * TOK], F32, kind="ExternalInput").ap()
        d["wxp"] = dt("wxp", [D, XK], BF16, kind="ExternalInput").ap()
        d["wdt"] = dt("wdt", [DTR, D], BF16, kind="ExternalInput").ap()
        d["alog"] = dt("alog", [P, NC * NS], F32, kind="ExternalInput").ap()
        d["bdt"] = dt("bdt", [P, NC], F32, kind="ExternalInput").ap()
        d["dpar"] = dt("dpar", [P, NC], F32, kind="ExternalInput").ap()
        d["gate_l"] = dt("gate_l", [P, NC * E], BF16, kind="ExternalInput").ap()
        d["wgu"] = dt("wgu", [E * 2 * NU, P, NC * P], BF16,
                      kind="ExternalInput").ap()
        d["wdn"] = dt("wdn", [E * (NC // 2), P, NU * 2 * P], BF16,
                      kind="ExternalInput").ap()
        d["sgu"] = dt("sgu", [2 * (SI // P), P, NC * P], BF16,
                      kind="ExternalInput").ap()
        d["sdn"] = dt("sdn", [NQ * (NC // 2), P, NU * 2 * P], BF16,
                      kind="ExternalInput").ap()
        d["sgv_l"] = dt("sgv_l", [P, NC], BF16, kind="ExternalInput").ap()
        d["e16"] = dt("e16", [16, 16 * P], BF16, kind="ExternalInput").ap()
        d["ident"] = dt("ident", [P, P], F32, kind="ExternalInput").ap()
        d["onesb"] = dt("onesb", [P, 1], BF16, kind="ExternalInput").ap()
        d["onesr"] = dt("onesr", [1, P], F32, kind="ExternalInput").ap()
        d["selin"] = dt("selin", [P, 2 * NPREV], F32, kind="ExternalInput").ap()
        d["out_fm"] = dt("out_fm", [D, TOK], F32, kind="ExternalOutput").ap()
        if cfg.get("DEBUG"):
            d["dbg_hin"] = dt("dbg_hin", [P, NC * NS], F32, kind="ExternalOutput").ap()
            d["dbg_sacc"] = dt("dbg_sacc", [P, NC], F32, kind="ExternalOutput").ap()
            for r in range(NPREV):
                d[f"dbg_ar{r}"] = dt(f"dbg_ar{r}", [P, NC * NS], F32, kind="ExternalOutput").ap()
                d[f"dbg_br{r}"] = dt(f"dbg_br{r}", [P, NC * NS], F32, kind="ExternalOutput").ap()
        d["xn_dram"] = dt("xn_dram", [NT, P, NC * TT], BF16).ap()
        d["y_dram"] = dt("y_dram", [D, TOK], F32).ap()
        d["acc_dram"] = dt("acc_dram", [D, TOK], F32).ap()

        st_ctx = contextlib.ExitStack()
        with st_ctx:
            def sb(name, shape, dty):
                return st_ctx.enter_context(nc.sbuf_tensor(name, shape, dty))
            h = self.h = {}
            h["wxp_s"] = sb("wxp_s", [P, NC * XK], BF16)
            h["wdt_s"] = sb("wdt_s", [P, NKD * D], BF16)
            h["gate_s"] = sb("gate_s", [P, NC * E], BF16)
            h["sgv_s"] = sb("sgv_s", [P, NC], BF16)
            h["alog_s"] = sb("alog_s", [P, NC * NS], F32)
            h["bdt_s"] = sb("bdt_s", [P, NC], F32)
            h["d_s"] = sb("d_s", [P, NC], F32)
            h["e16_s"] = sb("e16_s", [16, 16 * P], BF16)
            h["id_s"] = sb("id_s", [P, P], F32)
            h["onesb_s"] = sb("onesb_s", [P, 1], BF16)
            h["onesr_s"] = sb("onesr_s", [1, P], F32)
            h["sel_s"] = sb("sel_s", [P, 2 * NPREV], F32)
            h["cvals"] = sb("cvals", [P, 4], F32)
            h["x_sl"] = [sb(f"x{i}", [P, TT], F32) for i in range(2)]
            h["xbf_sl"] = [sb(f"xbf{i}", [P, TT], BF16) for i in range(2)]
            h["xsq_sl"] = [sb(f"xsq{i}", [P, TT], BF16) for i in range(2)]
            h["xn_t"] = sb("xn_t", [P, NC * TT], BF16)
            h["arep_s"] = sb("arep_s", [P, TT], F32)
            h["brep_s"] = sb("brep_s", [P, TT], F32)
            h["dtr_s"] = sb("dtr_s", [P, NKD * TT], BF16)
            h["bc_s"] = sb("bc_s", [16, TT], BF16)
            h["ball_s"] = sb("ball_s", [P, NS * TT], BF16)
            h["call_s"] = sb("call_s", [P, NS * TT], BF16)
            h["delta_sl"] = [sb(f"delta{i}", [P, TT], F32) for i in range(2)]
            h["sfpA"] = sb("sfpA", [P, TT], F32)
            h["sfpB"] = sb("sfpB", [P, TT], F32)
            h["u_s"] = sb("u_s", [P, TT], F32)
            h["dA_sl"] = [sb(f"dA{i}", [P, TT], F32) for i in range(2)]
            h["dBu_sl"] = [sb(f"dBu{i}", [P, TT], F32) for i in range(2)]
            h["h_all"] = sb("h_all", [P, NS * TT], F32)
            h["y_sl"] = [sb("y0", [P, TT], F32)]
            rows3 = sb("rows3", [1, 3 * TT], F32)
            h["rows"] = {"m": rows3[:, 0:TT], "var": rows3[:, TT:2 * TT],
                         "tmp": rows3[:, 2 * TT:3 * TT]}
            h["spart"] = sb("spart", [P, 1], F32)
            h["sacc"] = sb("sacc", [P, NC], F32)
            h["carry"] = sb("carry", [P, NC * NS], F32)
            h["ar_s"] = [sb(f"ar{r}", [P, NC * NS], F32) for r in range(NPREV)]
            h["br_s"] = [sb(f"br{r}", [P, NC * NS], F32) for r in range(NPREV)]
            h["wgu_sl"] = [sb(f"wgu{i}", [P, (NC // 2) * P], BF16)
                           for i in range(4)]
            h["wd_sl"] = [sb(f"wd{i}", [P, NU * 2 * P], BF16) for i in range(NWD)]
            h["xnf_s"] = sb("xnf_s", [P, NC * TT], BF16)
            h["acc_s"] = sb("acc_s", [P, NC * TT], F32)
            h["h_s"] = sb("h_s", [P, NU * TT], BF16)
            h["silu_s"] = sb("silu_s", [P, TT], BF16)
            h["upsc_s"] = sb("upsc_s", [P, TT], BF16)
            h["cw_s"] = sb("cw_s", [P, TT], BF16)
            h["grep_s"] = sb("grep_s", [P, TT], BF16)
            h["logit_s"] = sb("logit_s", [16, TT], F32)
            NG = TT // P
            h["ltm_s"] = sb("ltm_s", [P, NG * 16], F32)
            h["cur_s"] = sb("cur_s", [P, NG * 16], F32)
            h["ex_s"] = sb("ex_s", [P, NG * 16], F32)
            h["msk_s"] = sb("msk_s", [P, NG * 16], F32)
            h["neg1_s"] = sb("neg1_s", [P, NG], F32)
            h["mk_s"] = sb("mk_s", [P, NG * 8], F32)
            h["ssum_s"] = sb("ssum_s", [P, NG], F32)
            h["comb_s"] = sb("comb_s", [16, TT], BF16)
            h["fa_sl"] = [sb("fa0", [P, TT], F32)]
            h["fy_sl"] = [sb("fy0", [P, TT], F32)]
            h["fx_sl"] = [sb("fx0", [P, TT], F32)]
            h["a_s"] = h["alog_s"]
            h["zsum"] = h["h_all"][:, 0:NC * NS]
            h["banks"] = [st_ctx.enter_context(
                nc.psum_tensor(f"bank{i}", [P, 512], F32)) for i in range(8)]

            semnames = (["pe", "dve", "act", "pool", "sp",
                         "x0", "x1", "xst0", "xst1", "xnld",
                         "yw0", "accst", "outst0", "const",
                         "fa0", "fy0", "fx0"] +
                        [f"wgu{i}" for i in range(4)] +
                        [f"wd{i}" for i in range(NWD)])
            for n in semnames:
                self.mksem(n, st_ctx)

            self.bk = [self.trk() for _ in range(8)]
            for nm in ["t_xnf", "t_acc", "t_h", "t_rows", "t_arep", "t_xn_t",
                       "t_dtr", "t_bc", "t_ball", "t_call", "t_u", "t_hall",
                       "t_silu", "t_upsc", "t_cw", "t_grep", "t_logit",
                       "t_ltm", "t_comb"]:
                setattr(self, nm, self.trk())
            for nm in ["t_x", "t_xbf", "t_xsq", "t_delta", "t_dA", "t_dBu",
                       "t_y", "t_fa", "t_fy", "t_fx"]:
                setattr(self, nm, [self.trk() for _ in range(2)])
            self.t_wgu = [self.trk() for _ in range(4)]
            self.t_wd = [self.trk() for _ in range(NWD)]
            self.t_sfp = [self.trk() for _ in range(3)]
            self.xst_done = [None] * NT
            self.yw_done = [None] * NT
            self.accst_done = [None] * NT
            self.wgu_rr = 0
            self.wd_rr = 0
            self.mm_rr = 0
            self.h_rvs = {}
            self.acc_dep = []
            self.summ_done = []
            self.sacc_frees = []

            self.emit_init()
            if cfg["OVERLAP"]:
                self.emit_attn_tile("xn", None, 0)
                y_walked = set()
                ffn_walked = set()
                fin_done = set()

                def attn_stream():
                    for t in range(1, NT):
                        yield from self.gen_attn_tile("xn", None, t)
                    for r in range(NPREV):
                        for t in range(NT):
                            yield from self.gen_attn_tile("carry", r, t)
                    self.emit_hin()
                    yield
                    for t in range(NT):
                        yield from self.gen_attn_tile("y", None, t)
                        y_walked.add(t)
                        yield

                def ffn_stream():
                    for s in range(NT):
                        yield from self.gen_ffn_st(s)
                        ffn_walked.add(s)
                        yield

                ag, fg = attn_stream(), ffn_stream()
                a_more = f_more = True
                i = 0
                while a_more or f_more:
                    i += 1
                    if f_more:
                        for _ in range(2 + (i % 2)):
                            try:
                                next(fg)
                            except StopIteration:
                                f_more = False
                                break
                    if a_more:
                        try:
                            next(ag)
                        except StopIteration:
                            a_more = False
                    for t in range(NT):
                        if (t in y_walked and t in ffn_walked
                                and t not in fin_done):
                            self.emit_fin_st(t)
                            fin_done.add(t)
                for t in range(NT):
                    if t not in fin_done:
                        self.emit_fin_st(t)
                        fin_done.add(t)
            else:
                for t in range(NT):
                    self.emit_attn_tile("xn", None, t)
                for r in range(NPREV):
                    for t in range(NT):
                        self.emit_attn_tile("carry", r, t)
                self.emit_hin()
                for t in range(NT):
                    self.emit_attn_tile("y", None, t)
                for s in range(NT):
                    for _ in self.gen_ffn_st(s):
                        pass
                for s in range(NT):
                    self.emit_fin_st(s)
            self.emit_end()

            with nc.Block() as block:
                @block.sync
                def _(e):
                    for f in self.q["sync"]:
                        f(e)

                @block.scalar
                def _(e):
                    for f in self.q["scalar"]:
                        f(e)

                @block.vector
                def _(e):
                    for f in self.q["vector"]:
                        f(e)

                @block.tensor
                def _(e):
                    for f in self.q["tensor"]:
                        f(e)

                @block.gpsimd
                def _(e):
                    for f in self.q["gpsimd"]:
                        f(e)
        return nc

    def emit_init(self):
        h, d = self.h, self.d
        loads = [
            ("wxp_s", "wxp", "(c p) k -> p c k"),
            ("wdt_s", "wdt", "(k p) dd -> p k dd"),
            ("gate_s", "gate_l", None),
            ("sgv_s", "sgv_l", None),
            ("alog_s", "alog", None), ("bdt_s", "bdt", None),
            ("d_s", "dpar", None), ("e16_s", "e16", None),
            ("id_s", "ident", None), ("onesb_s", "onesb", None),
            ("onesr_s", "onesr", None), ("sel_s", "selin", None),
        ]
        for sbn, dn, pat in loads:
            src = d[dn] if pat is None else d[dn].rearrange(pat, p=P)
            dst = h[sbn][:, :]
            if pat is not None:
                nk = src.shape[1]
                dst = dst.rearrange("p (a b) -> p a b", a=nk)
            self.dma("gpsimd", dst, src, "const")
        tot = self.cnt["const"]
        self.const_done = [("const", tot)]
        self.eop("scalar", lambda e: e.activation(
            out=h["alog_s"][:, :], in_=h["alog_s"][:, :], func=AF.Exp),
            waits=self.const_done)
        self.a_ready = self.eop("vector", lambda e: e.tensor_scalar_mul(
            h["a_s"][:, :], h["a_s"][:, :], -1.0),
            waits=self.now("act"))
        self.carry_last = self.eop("vector", lambda e: e.memset(
            h["carry"][:, :], 0.0))
        for j, v in [(0, 1e-5), (1, -1.0), (2, 1.0)]:
            self.cv_ready = self.eop("vector", lambda e, j=j, v=v: e.memset(
                h["cvals"][:, j:j + 1], v))

    def emit_attn_tile(self, kind, rng, t):
        for _ in self.gen_attn_tile(kind, rng, t):
            pass

    def gen_attn_tile(self, kind, rng, t):
        h, d, cfg = self.h, self.d, self.cfg
        D, TT, NS = cfg["D"], cfg["TT"], cfg["NS"]
        NC, NKD = D // P, cfg["DTR"] // P
        XK = cfg["DTR"] + 2 * NS
        NT = cfg["TOK"] // TT
        do_scan = kind in ("carry", "y")
        do_y = kind == "y"
        if kind == "carry":
            src, col0 = d["x_prev"], rng * cfg["TOK"] + t * TT
        else:
            src, col0 = d["x_own"], t * TT

        if kind == "carry" and t == 0:
            self.sacc_last = self.eop("gpsimd", lambda e: e.memset(
                h["sacc"][:, :], 0.0), waits=self.sacc_frees)
            self.sacc_frees = []

        bsum, bsq = self.bk[5], self.bk[6]
        pe_rv = None
        for c in range(NC):
            sl = c % 2
            xs, tx = h["x_sl"][sl], self.t_x[sl]
            rv = self.dma("gpsimd", xs[:, :],
                          src[c * P:(c + 1) * P, col0:col0 + TT],
                          f"x{sl}", waits=tx["frees"] + self.const_done)
            self.produce(tx, rv)
            xbf, txb = h["xbf_sl"][sl], self.t_xbf[sl]
            rb = self.eop("gpsimd", lambda e, o=xbf, i=xs: e.tensor_copy(
                out=o[:, :], in_=i[:, :]), waits=tx["ready"] + txb["frees"])
            self.produce(txb, rb)
            self.consume(tx, rb)
            xsq, tsq = h["xsq_sl"][sl], self.t_xsq[sl]
            rq = self.eop("gpsimd", lambda e, o=xsq, i=xs: e.tensor_mul(
                o[:, :], i[:, :], i[:, :]),
                waits=tsq["frees"])
            self.produce(tsq, rq)
            self.consume(tx, rq)
            w_extra = (bsum["frees"] + bsq["frees"]) if c == 0 else []
            self.eop("tensor", lambda e, r=xbf, c_=c: e.matmul(
                h["banks"][5][0:1, 0:TT], h["onesb_s"][:, :], r[:, :],
                start=(c_ == 0), stop=(c_ == NC - 1)),
                waits=txb["ready"] + w_extra, inc=False)
            pe_rv = self.eop("tensor", lambda e, r=xsq, c_=c: e.matmul(
                h["banks"][6][0:1, 0:TT], h["onesb_s"][:, :], r[:, :],
                start=(c_ == 0), stop=(c_ == NC - 1)),
                waits=tsq["ready"], inc=(c == NC - 1))
            self.consume(txb, pe_rv)
            self.consume(tsq, pe_rv)
            if c % 4 == 3 and c != NC - 1:
                yield
        self.produce(bsum, pe_rv)
        self.produce(bsq, pe_rv)
        R = h["rows"]
        tr = self.t_rows
        rv = self.eop("vector", lambda e: e.tensor_scalar_mul(
            R["m"][:, :], h["banks"][5][0:1, 0:TT], 1.0 / D),
            waits=bsum["ready"] + tr["frees"])
        self.consume(bsum, rv)
        rv = self.eop("vector", lambda e: e.tensor_scalar_mul(
            R["var"][:, :], h["banks"][6][0:1, 0:TT], 1.0 / D),
            waits=bsq["ready"])
        self.consume(bsq, rv)
        self.eop("vector", lambda e: e.tensor_mul(
            R["tmp"], R["m"], R["m"]))
        self.eop("vector", lambda e: e.tensor_sub(
            R["var"], R["var"], R["tmp"]))
        rv = self.eop("scalar", lambda e: e.activation(
            out=R["tmp"], in_=R["var"], func=AF.Sqrt,
            bias=h["cvals"][0:1, 0:1]), waits=self.now("dve"))
        rv = self.eop("vector", lambda e: e.reciprocal(
            R["var"], R["tmp"]), waits=[rv])
        rv = self.eop("vector", lambda e: e.scalar_tensor_tensor(
            out=R["tmp"], in0=R["m"], scalar=-1.0,
            in1=R["var"], op0=ALU.mult, op1=ALU.mult))
        self.produce(tr, rv)
        b5, b6 = self.bk[5], self.bk[6]
        self.eop("tensor", lambda e: e.matmul(
            h["banks"][5][:, 0:TT], h["onesr_s"][:, :], R["var"],
            start=True, stop=True), waits=[rv] + b5["frees"], inc=False)
        pe_rv = self.eop("tensor", lambda e: e.matmul(
            h["banks"][6][:, 0:TT], h["onesr_s"][:, :], R["tmp"],
            start=True, stop=True), waits=b6["frees"], inc=True)
        self.produce(b5, pe_rv)
        self.produce(b6, pe_rv)
        ta = self.t_arep
        rv1 = self.eop("scalar", lambda e: e.copy(
            out=h["arep_s"][:, :], in_=h["banks"][5][:, 0:TT]),
            waits=b5["ready"] + ta["frees"])
        self.consume(b5, rv1)
        rv2 = self.eop("scalar", lambda e: e.copy(
            out=h["brep_s"][:, :], in_=h["banks"][6][:, 0:TT]),
            waits=b6["ready"])
        self.consume(b6, rv2)
        self.consume(tr, rv2)
        ta["ready"] = [rv2]
        ta["frees"] = []
        yield

        txn = self.t_xn_t
        xdbl = []
        m0 = 0
        bi = 5
        while m0 < XK and do_scan:
            xdbl.append((bi, m0, min(P, XK - m0)))
            m0 += P
            bi += 1
        for c in range(NC):
            sl = c % 2
            xs, tx = h["x_sl"][sl], self.t_x[sl]
            rv = self.dma("gpsimd", xs[:, :],
                          src[c * P:(c + 1) * P, col0:col0 + TT],
                          f"x{sl}", waits=tx["frees"])
            self.produce(tx, rv)
            tA = self.t_sfp[0]
            self.eop("gpsimd", lambda e, i=xs: e.tensor_mul(
                h["sfpA"][:, :], i[:, :], h["arep_s"][:, :]),
                waits=tx["ready"] + ta["ready"] + tA["frees"])
            rv = self.eop("gpsimd", lambda e: e.tensor_add(
                h["sfpA"][:, :], h["sfpA"][:, :], h["brep_s"][:, :]))
            self.consume(tx, rv)
            self.produce(tA, rv)
            rvc = self.eop("gpsimd", lambda e, c_=c: e.tensor_copy(
                out=h["xn_t"][:, c_ * TT:(c_ + 1) * TT], in_=h["sfpA"][:, :]),
                waits=tA["ready"] + (txn["frees"] if c == 0 else []))
            self.consume(tA, rvc)
            if c == 0:
                txn["frees"] = []
            for bi_, mb, mlen in xdbl:
                w_extra = (self.bk[bi_]["frees"] if c == 0 else [])
                rvp = self.eop("tensor", lambda e, c_=c, bi_=bi_, mb=mb,
                               mlen=mlen: e.matmul(
                    h["banks"][bi_][0:mlen, 0:TT],
                    h["wxp_s"][:, c_ * XK + mb:c_ * XK + mb + mlen],
                    h["xn_t"][:, c_ * TT:(c_ + 1) * TT],
                    start=(c_ == 0), stop=(c_ == NC - 1)),
                    waits=[rvc] + w_extra + self.const_done,
                    inc=(c == NC - 1 and bi_ == xdbl[-1][0]))
            if do_scan and c == NC - 1:
                for bi_, _, _ in xdbl:
                    self.produce(self.bk[bi_], rvp)
            if c % 2 == 1 and c != NC - 1:
                yield
        txn["ready"] = [("pool", self.cnt["pool"])]

        if kind == "xn":
            sl = t % 2
            rv = self.dma("gpsimd", d["xn_dram"][t], h["xn_t"][:, :],
                          f"xst{sl}", waits=txn["ready"])
            self.xst_done[t] = rv
            txn["frees"] = [rv, ("pool", self.cnt["pool"])]
            return

        tdtr, tbc = self.t_dtr, self.t_bc
        for ki in range(NKD):
            bi_ = 5 + ki
            rv = self.eop("scalar", lambda e, ki=ki, bi_=bi_: e.copy(
                out=h["dtr_s"][:, ki * TT:(ki + 1) * TT],
                in_=h["banks"][bi_][:, 0:TT]),
                waits=self.bk[bi_]["ready"] + (tdtr["frees"] if ki == 0 else []))
            self.consume(self.bk[bi_], rv)
        tdtr["ready"] = [rv]
        tdtr["frees"] = []
        bcb = 5 + NKD
        nbc = 2 * NS
        rv = self.eop("scalar", lambda e, bcb=bcb: e.copy(
            out=h["bc_s"][:, :], in_=h["banks"][bcb][0:nbc, 0:TT]),
            waits=self.bk[bcb]["ready"] + tbc["frees"])
        self.consume(self.bk[bcb], rv)
        self.produce(tbc, rv)

        sel_jobs = [(0, "ball_s", self.t_ball)]
        if do_y:
            sel_jobs.append((1, "call_s", self.t_call))
        for which, tgt, ttrk in sel_jobs:
            for n in range(NS):
                bi_ = 7 if (n % 2 == 0) else 5
                j = n + NS * which
                rvp = self.eop("tensor", lambda e, j=j, bi_=bi_: e.matmul(
                    h["banks"][bi_][:, 0:TT], h["e16_s"][:, j * P:(j + 1) * P],
                    h["bc_s"][:, :], start=True, stop=True),
                    waits=tbc["ready"] + self.bk[bi_]["frees"], inc=True)
                self.produce(self.bk[bi_], rvp)
                rv = self.eop("scalar", lambda e, n=n, tgt=tgt, bi_=bi_: e.copy(
                    out=h[tgt][:, n * TT:(n + 1) * TT],
                    in_=h["banks"][bi_][:, 0:TT]),
                    waits=self.bk[bi_]["ready"] + (ttrk["frees"] if n == 0 else []))
                self.consume(self.bk[bi_], rv)
            ttrk["ready"] = [rv]
            ttrk["frees"] = []
            yield
        self.consume(tbc, rv)

        for c in range(NC):
            dbank = 5 if (c % 2 == 0) else 6
            dsl = c % 2
            dsb, tds = h["delta_sl"][dsl], self.t_delta[dsl]
            for ki in range(NKD):
                rvp = self.eop("tensor", lambda e, c_=c, ki=ki, bi_=dbank:
                               e.matmul(
                    h["banks"][bi_][:, 0:TT],
                    h["wdt_s"][:, ki * D + c_ * P:ki * D + (c_ + 1) * P],
                    h["dtr_s"][:, ki * TT:(ki + 1) * TT],
                    start=(ki == 0), stop=(ki == NKD - 1)),
                    waits=tdtr["ready"] +
                    (self.bk[dbank]["frees"] if ki == 0 else []),
                    inc=(ki == NKD - 1))
            self.produce(self.bk[dbank], rvp)
            tA, tB, tR = self.t_sfp
            bcol = h["bdt_s"][:, c:c + 1]
            rv1 = self.eop("scalar", lambda e, bi_=dbank, bcol=bcol:
                           e.activation(
                out=h["sfpA"][:, :], in_=h["banks"][bi_][:, 0:TT], func=AF.Abs,
                bias=bcol), waits=self.bk[dbank]["ready"] + tA["frees"])
            self.produce(tA, rv1)
            rv2 = self.eop("scalar", lambda e: e.activation(
                out=h["sfpB"][:, :], in_=h["sfpA"][:, :], func=AF.Exp,
                scale=h["cvals"][:, 1:2]), waits=tB["frees"] + tR["frees"])
            self.consume(tA, rv2)
            self.produce(tB, rv2)
            rv3 = self.eop("scalar", lambda e: e.activation(
                out=h["sfpA"][:, :], in_=h["sfpB"][:, :], func=AF.Ln,
                bias=h["cvals"][:, 2:3]))
            self.consume(tB, rv3)
            self.produce(tA, rv3)
            rv4 = self.eop("vector", lambda e, bi_=dbank, bcol=bcol:
                           e.tensor_scalar(
                out=h["sfpB"][:, :], in0=h["banks"][bi_][:, 0:TT],
                scalar1=bcol, scalar2=0.0, op0=ALU.add, op1=ALU.max),
                waits=self.bk[dbank]["ready"] + tR["frees"] + [rv3])
            self.consume(self.bk[dbank], rv4)
            self.produce(tR, rv4)
            if kind == "carry":
                rv5 = self.eop("vector", lambda e, dsb=dsb:
                               e.scalar_tensor_tensor(
                    out=dsb[:, :], in0=h["sfpB"][:, :], scalar=1.0,
                    in1=h["sfpA"][:, :], op0=ALU.mult, op1=ALU.add,
                    accum_out=h["spart"][:, :]),
                    waits=[rv3] + tds["frees"])
                self.sacc_last = self.eop("gpsimd", lambda e, c_=c:
                                          e.tensor_add(
                    h["sacc"][:, c_:c_ + 1], h["sacc"][:, c_:c_ + 1],
                    h["spart"][:, :]), waits=[rv5])
            else:
                rv5 = self.eop("gpsimd", lambda e, dsb=dsb: e.tensor_add(
                    dsb[:, :], h["sfpB"][:, :], h["sfpA"][:, :]),
                    waits=[rv3, rv4] + tds["frees"])
            self.consume(tA, rv5)
            self.consume(tR, rv5)
            self.produce(tds, rv5)
            tu = self.t_u
            rvu = self.eop("gpsimd", lambda e, c_=c, dsb=dsb: e.tensor_mul(
                h["u_s"][:, :], dsb[:, :], h["xn_t"][:, c_ * TT:(c_ + 1) * TT]),
                waits=tds["ready"] + tu["frees"] + txn["ready"])
            self.produce(tu, rvu)
            self.consume(tds, rvu)
            for n in range(NS):
                asl = n % 2
                dAb, tdA = h["dA_sl"][asl], self.t_dA[asl]
                dBb, tdB = h["dBu_sl"][asl], self.t_dBu[asl]
                acol = h["a_s"][:, c * NS + n:c * NS + n + 1]
                rva = self.eop("scalar", lambda e, dsb=dsb, dAb=dAb, acol=acol:
                               e.activation(out=dAb[:, :], in_=dsb[:, :],
                                            func=AF.Exp, scale=acol),
                               waits=tds["ready"] + tdA["frees"] +
                               [self.a_ready])
                self.produce(tdA, rva)
                rvb = self.eop("gpsimd", lambda e, n=n, dBb=dBb: e.tensor_mul(
                    dBb[:, :], h["u_s"][:, :],
                    h["ball_s"][:, n * TT:(n + 1) * TT]),
                    waits=tu["ready"] + tdB["frees"] + self.t_ball["ready"])
                self.produce(tdB, rvb)
                ic = h["carry"][:, c * NS + n:c * NS + n + 1]
                dst = h["h_all"][:, n * TT:(n + 1) * TT]
                rvs = self.eop("vector", lambda e, dAb=dAb, dBb=dBb, ic=ic,
                               dst=dst: e.tensor_tensor_scan(
                    out=dst, data0=dAb[:, :], data1=dBb[:, :], initial=ic,
                    op0=ALU.mult, op1=ALU.add),
                    waits=tdA["ready"] + tdB["ready"] +
                    ([self.carry_last] if (n == 0 and c == 0) else []) +
                    (self.t_hall["frees"] if (n == 0 and c == 0) else []))
                self.consume(tdA, rvs)
                self.consume(tdB, rvs)
                if n == NS - 1:
                    self.consume(tu, rvs)
            self.carry_last = self.eop("gpsimd", lambda e, c_=c: e.tensor_copy(
                out=h["carry"][:, c_ * NS:(c_ + 1) * NS],
                in_=h["h_all"][:, :].rearrange("p (n l) -> p n l", n=NS)
                [:, :, TT - 1]), waits=self.now("dve"))
            if do_y:
                self.eop("gpsimd", lambda e: e.tensor_mul(
                    h["h_all"][:, :], h["h_all"][:, :], h["call_s"][:, :]),
                    waits=self.t_call["ready"] + self.now("dve"))
                half = NS * TT // 2
                while half >= TT:
                    self.eop("gpsimd", lambda e, hf=half: e.tensor_add(
                        h["h_all"][:, 0:hf], h["h_all"][:, 0:hf],
                        h["h_all"][:, hf:2 * hf]))
                    half //= 2
                ysl = 0
                yb, ty = h["y_sl"][0], self.t_y[0]
                dcol = h["d_s"][:, c:c + 1]
                self.eop("gpsimd", lambda e, c_=c, yb=yb, dcol=dcol:
                         e.tensor_scalar(
                    out=yb[:, :], in0=h["xn_t"][:, c_ * TT:(c_ + 1) * TT],
                    scalar1=dcol, scalar2=None, op0=ALU.mult),
                    waits=ty["frees"] + txn["ready"])
                rvy = self.eop("gpsimd", lambda e, yb=yb: e.tensor_add(
                    yb[:, :], yb[:, :], h["h_all"][:, 0:TT]))
                self.produce(ty, rvy)
                yield
                rvd = self.dma("gpsimd",
                               d["y_dram"][c * P:(c + 1) * P, col0:col0 + TT],
                               yb[:, :], "yw0", waits=ty["ready"])
                ty["frees"].append(rvd)
            else:
                yield
        if do_y:
            self.yw_done[t] = [("yw0", self.cnt["yw0"])]
        endm = self.now("dve", "pe", "act", "pool")
        for tt_ in [txn, self.t_ball, self.t_call, ta, tdtr]:
            tt_["frees"] = list(endm)
        self.t_hall["frees"] = self.now("dve", "pool")

        if kind == "carry" and t == NT - 1:
            r = rng
            self.eop("gpsimd", lambda e, r=r: e.tensor_copy(
                out=h["br_s"][r][:, :], in_=h["carry"][:, :]))
            NCc = D // P
            rvz = self.eop("gpsimd", lambda e, NCc=NCc: e.tensor_tensor(
                out=h["zsum"].rearrange("p (c n) -> p c n", c=NCc),
                in0=h["sacc"][:, :].unsqueeze(2).broadcast_to([P, NCc, NS]),
                in1=h["a_s"][:, :].rearrange("p (c n) -> p c n", c=NCc),
                op=ALU.mult),
                waits=[self.sacc_last, self.a_ready] + self.now("dve"))
            self.sacc_frees = [rvz]
            rva = self.eop("scalar", lambda e, r=r: e.activation(
                out=h["ar_s"][r][:, :], in_=h["zsum"], func=AF.Exp),
                waits=[rvz])
            self.summ_done += [rva, rvz]
            self.t_hall["frees"].append(rva)
            if self.cfg.get("DEBUG"):
                self.dma("gpsimd", d[f"dbg_ar{r}"], h["ar_s"][r][:, :], "const",
                         waits=[rva])
                self.dma("gpsimd", d[f"dbg_br{r}"], h["br_s"][r][:, :], "const",
                         waits=self.now("dve"))
                if r == 0:
                    self.dma("gpsimd", d["dbg_sacc"], h["sacc"][:, :], "const",
                             waits=[self.sacc_last])
            self.carry_last = self.eop("gpsimd", lambda e: e.memset(
                h["carry"][:, :], 0.0))

    def emit_hin(self):
        h = self.h
        NPREV = self.cfg["NPREV"]
        rv = None
        for r in range(NPREV):
            sel = h["sel_s"][:, r:r + 1]
            isel = h["sel_s"][:, NPREV + r:NPREV + r + 1]
            self.eop("gpsimd", lambda e, r=r, sel=sel, isel=isel:
                     e.tensor_scalar(
                out=h["zsum"], in0=h["ar_s"][r][:, :], scalar1=sel,
                scalar2=isel, op0=ALU.mult, op1=ALU.add),
                waits=self.summ_done + [self.carry_last] + self.now("dve"))
            self.eop("gpsimd", lambda e: e.tensor_mul(
                h["carry"][:, :], h["carry"][:, :], h["zsum"]))
            self.eop("gpsimd", lambda e, r=r, sel=sel: e.tensor_scalar(
                out=h["zsum"], in0=h["br_s"][r][:, :], scalar1=sel,
                scalar2=None, op0=ALU.mult))
            rv = self.eop("gpsimd", lambda e: e.tensor_add(
                h["carry"][:, :], h["carry"][:, :], h["zsum"]))
        self.carry_last = rv
        if self.cfg.get("DEBUG"):
            self.dma("gpsimd", self.d["dbg_hin"], self.h["carry"][:, :], "const",
                     waits=[rv])
            self.wait("gpsimd", "const", self.cnt["const"])

    def gen_ffn_st(self, s):
        h, d, cfg = self.h, self.d, self.cfg
        D, TT, E, I, SI = cfg["D"], cfg["TT"], cfg["E"], cfg["I"], cfg["SI"]
        NC, NU, NQ = D // P, I // P, SI // I
        NG = TT // P
        col0 = s * TT

        txnf = self.t_xnf
        rv = self.dma("scalar", h["xnf_s"][:, :], d["xn_dram"][s],
                      "xnld", waits=txnf["frees"] + [self.xst_done[s]])
        self.produce(txnf, rv)

        def xn_chunk(c):
            return h["xnf_s"][:, c * TT:(c + 1) * TT]

        tacc = self.t_acc
        acc_w = list(tacc["frees"])

        b4 = self.bk[4]
        for c in range(NC):
            rvp = self.eop("tensor", lambda e, c_=c: e.matmul(
                h["banks"][4][0:16, 0:TT], h["gate_s"][:, c_ * E:(c_ + 1) * E],
                xn_chunk(c_), start=(c_ == 0), stop=(c_ == NC - 1)),
                waits=txnf["ready"] + (b4["frees"] if c == 0 else []),
                inc=(c == NC - 1))
        self.produce(b4, rvp)
        tl = self.t_logit
        rv = self.eop("scalar", lambda e: e.copy(
            out=h["logit_s"][:, :], in_=h["banks"][4][0:16, 0:TT]),
            waits=b4["ready"] + tl["frees"])
        self.consume(b4, rv)
        self.produce(tl, rv)
        tltm = self.t_ltm
        for g in range(NG):
            rvp = self.eop("tensor", lambda e, g=g: e.transpose(
                h["banks"][4][:, 0:16], h["logit_s"][:, g * P:(g + 1) * P],
                h["id_s"][0:16, 0:16]),
                waits=tl["ready"] + b4["frees"], inc=True)
            self.produce(b4, rvp)
            rv = self.eop("scalar", lambda e, g=g: e.copy(
                out=h["ltm_s"][:, g * 16:(g + 1) * 16],
                in_=h["banks"][4][:, 0:16]),
                waits=b4["ready"] + (tltm["frees"] if g == 0 else []))
            self.consume(b4, rv)
        self.consume(tl, rv)
        act_ltm = rv
        K4 = cfg["TOPK"]
        for g in range(NG):
            L = h["ltm_s"][:, g * 16:(g + 1) * 16]
            C = h["cur_s"][:, g * 16:(g + 1) * 16]
            t8 = h["mk_s"][:, g * 8:(g + 1) * 8]
            self.eop("vector", lambda e, L=L, t8=t8: e.max(
                out=t8, in_=L), waits=[act_ltm])
            n1 = h["neg1_s"][:, g:g + 1]
            self.eop("vector", lambda e, t8=t8, n1=n1: e.tensor_scalar_mul(
                n1, t8[:, 0:1], -1.0))
            rve = self.eop("scalar", lambda e, L=L, g=g, n1=n1: e.activation(
                out=h["ex_s"][:, g * 16:(g + 1) * 16], in_=L, func=AF.Exp,
                bias=n1), waits=self.now("dve"))
            self.eop("vector", lambda e, t8=t8: e.memset(
                t8[:, K4:8], 1e30))
            self.eop("vector", lambda e, L=L, C=C, t8=t8: e.match_replace(
                out=C, in_to_replace=t8, in_values=L, imm_value=NBIG))
            Mg = h["msk_s"][:, g * 16:(g + 1) * 16]
            self.eop("vector", lambda e, C=C, L=L, Mg=Mg: e.tensor_tensor(
                out=Mg, in0=C, in1=L, op=ALU.not_equal))
            self.eop("vector", lambda e, g=g, Mg=Mg: e.tensor_mul(
                Mg, Mg, h["ex_s"][:, g * 16:(g + 1) * 16]), waits=[rve])
            sm = h["ssum_s"][:, g:g + 1]
            self.eop("vector", lambda e, Mg=Mg, sm=sm: e.reduce_sum(
                out=sm, in_=Mg, axis=AX.X))
            self.eop("vector", lambda e, sm=sm: e.reciprocal(sm, sm))
            rv = self.eop("vector", lambda e, Mg=Mg, sm=sm: e.tensor_scalar_mul(
                Mg, Mg, sm))
        tc = self.t_comb
        for g in range(NG):
            rvp = self.eop("tensor", lambda e, g=g: e.transpose(
                h["banks"][4][0:16, 0:P], h["msk_s"][:, g * 16:(g + 1) * 16],
                h["id_s"][:, :]),
                waits=[rv] + b4["frees"], inc=True)
            self.produce(b4, rvp)
            rvc = self.eop("scalar", lambda e, g=g: e.copy(
                out=h["comb_s"][:, g * P:(g + 1) * P],
                in_=h["banks"][4][0:16, 0:P]),
                waits=b4["ready"] + (tc["frees"] if g == 0 else []))
            self.consume(b4, rvc)
        tc["ready"] = [rvc]
        tc["frees"] = []

        for c in range(NC):
            rvp = self.eop("tensor", lambda e, c_=c: e.matmul(
                h["banks"][4][0:1, 0:TT], h["sgv_s"][:, c_:c_ + 1],
                xn_chunk(c_), start=(c_ == 0), stop=(c_ == NC - 1)),
                waits=(b4["frees"] if c == 0 else []), inc=(c == NC - 1))
        self.produce(b4, rvp)
        rv = self.eop("scalar", lambda e: e.activation(
            out=h["logit_s"][0:1, 0:TT], in_=h["banks"][4][0:1, 0:TT],
            func=AF.Sigmoid), waits=b4["ready"])
        self.consume(b4, rv)
        rvp = self.eop("tensor", lambda e: e.matmul(
            h["banks"][4][:, 0:TT], h["onesr_s"][:, :], h["logit_s"][0:1, 0:TT],
            start=True, stop=True), waits=[rv] + b4["frees"], inc=True)
        self.produce(b4, rvp)
        tg = self.t_grep
        rv = self.eop("scalar", lambda e: e.copy(
            out=h["grep_s"][:, :], in_=h["banks"][4][:, 0:TT]),
            waits=b4["ready"] + tg["frees"])
        self.consume(b4, rv)
        self.produce(tg, rv)
        yield

        ROT = [0, 1, 2, 3, 4]

        def nextbank():
            bi = ROT[self.mm_rr % len(ROT)]
            self.mm_rr += 1
            return bi

        NCH = NC // 2

        def gu_unit(src_dram, unit):
            bks = []
            for half in range(2):
                bi = nextbank()
                bks.append(bi)
                for kh in range(2):
                    wsl = self.wgu_rr % 4
                    self.wgu_rr += 1
                    wt = self.t_wgu[wsl]
                    rv = self.dma("sync", h["wgu_sl"][wsl][:, :],
                                  src_dram[2 * unit + half]
                                  [:, kh * NCH * P:(kh + 1) * NCH * P],
                                  f"wgu{wsl}", waits=wt["frees"])
                    self.produce(wt, rv)
                    for cl in range(NCH):
                        c = kh * NCH + cl
                        rvp = self.eop("tensor", lambda e, c_=c, cl=cl, bi=bi,
                                       wsl=wsl: e.matmul(
                            h["banks"][bi][:, 0:TT],
                            h["wgu_sl"][wsl][:, cl * P:(cl + 1) * P],
                            xn_chunk(c_),
                            start=(c_ == 0), stop=(c_ == NC - 1)),
                            waits=wt["ready"] +
                            (self.bk[bi]["frees"] if c == 0 else []),
                            inc=(cl == NCH - 1))
                    self.consume(wt, rvp)
                self.produce(self.bk[bi], rvp)
            return bks

        def unit_epilogue(j_local, use_cw, bg, bu):
            ts_ = self.t_silu
            rv1 = self.eop("scalar", lambda e, bg=bg: e.activation(
                out=h["silu_s"][:, :], in_=h["banks"][bg][:, 0:TT],
                func=AF.Silu), waits=self.bk[bg]["ready"] + ts_["frees"])
            self.consume(self.bk[bg], rv1)
            self.produce(ts_, rv1)
            tu_ = self.t_upsc
            if use_cw:
                rv2 = self.eop("vector", lambda e, bu=bu:
                               e.scalar_tensor_tensor(
                    out=h["upsc_s"][:, :], in0=h["banks"][bu][:, 0:TT],
                    scalar=1.0, in1=h["cw_s"][:, :], op0=ALU.mult,
                    op1=ALU.mult),
                    waits=self.bk[bu]["ready"] + tu_["frees"] +
                    self.t_cw["ready"])
            else:
                rv2 = self.eop("vector", lambda e, bu=bu: e.tensor_copy(
                    out=h["upsc_s"][:, :], in_=h["banks"][bu][:, 0:TT]),
                    waits=self.bk[bu]["ready"] + tu_["frees"])
            self.consume(self.bk[bu], rv2)
            self.produce(tu_, rv2)
            rv3 = self.eop("vector", lambda e, j=j_local: e.tensor_mul(
                h["h_s"][:, j * TT:(j + 1) * TT], h["silu_s"][:, :],
                h["upsc_s"][:, :]),
                waits=[rv1] + (self.t_h["frees"] if j_local == 0 else []))
            self.consume(ts_, rv3)
            self.consume(tu_, rv3)
            self.h_rvs[j_local] = rv3
            return rv3

        def down_pass(src_dram, mode):
            h_rvs = dict(self.h_rvs)
            first = True
            for cg in range(NC // 2):
                wsl = self.wd_rr % len(h["wd_sl"])
                self.wd_rr += 1
                wt = self.t_wd[wsl]
                rv = self.dma("sync", h["wd_sl"][wsl][:, :],
                              src_dram[cg],
                              f"wd{wsl}", waits=wt["frees"])
                self.produce(wt, rv)
                for mi in range(2):
                    bi = nextbank()
                    c_out = cg * 2 + mi
                    for k in range(NU):
                        rvp = self.eop("tensor", lambda e, k=k, mi=mi, bi=bi,
                                       wsl=wsl: e.matmul(
                            h["banks"][bi][:, 0:TT],
                            h["wd_sl"][wsl][:, k * 2 * P + mi * P:
                                            k * 2 * P + (mi + 1) * P],
                            h["h_s"][:, k * TT:(k + 1) * TT],
                            start=(k == 0), stop=(k == NU - 1)),
                            waits=wt["ready"] + [h_rvs[k]] +
                            (self.bk[bi]["frees"] if k == 0 else []),
                            inc=(k == NU - 1))
                    self.produce(self.bk[bi], rvp)
                    if mi == 1:
                        self.consume(wt, rvp)
                    aw = (acc_w + self.acc_dep) if first else []
                    first = False
                    if mode == "init":
                        rva = self.eop("scalar", lambda e, c_=c_out, bi=bi:
                                       e.copy(
                            out=h["acc_s"][:, c_ * TT:(c_ + 1) * TT],
                            in_=h["banks"][bi][:, 0:TT]),
                            waits=self.bk[bi]["ready"] + aw)
                    else:
                        rva = self.eop("vector", lambda e, c_=c_out, bi=bi:
                                       e.tensor_add(
                            h["acc_s"][:, c_ * TT:(c_ + 1) * TT],
                            h["acc_s"][:, c_ * TT:(c_ + 1) * TT],
                            h["banks"][bi][:, 0:TT]),
                            waits=self.bk[bi]["ready"] + aw)
                    self.consume(self.bk[bi], rva)
                    self.last_down = rva
                if cg % 2 == 1:
                    yield

        for qq in range(NQ):
            rvh = None
            for j in range(NU):
                bg, bu = gu_unit(d["sgu"], qq * NU + j)
                rvh = unit_epilogue(j, use_cw=False, bg=bg, bu=bu)
                yield
            self.produce(self.t_h, rvh)
            yield from down_pass(
                d["sdn"][qq * (NC // 2):(qq + 1) * (NC // 2)],
                "init" if qq == 0 else "add")
            last = self.last_down
            self.t_h["frees"] = [last]
            self.acc_dep = [last]
        acc_sc = []
        for c in range(NC):
            rv = self.eop("vector", lambda e, c_=c: e.tensor_mul(
                h["acc_s"][:, c_ * TT:(c_ + 1) * TT],
                h["acc_s"][:, c_ * TT:(c_ + 1) * TT], h["grep_s"][:, :]),
                waits=tg["ready"] + self.acc_dep)
            acc_sc.append(rv)
        self.consume(tg, rv)
        self.acc_dep = [rv]

        for ei in range(E):
            rvp = self.eop("tensor", lambda e, ei=ei: e.matmul(
                h["banks"][4][:, 0:TT], h["e16_s"][:, ei * P:(ei + 1) * P],
                h["comb_s"][:, :], start=True, stop=True),
                waits=tc["ready"] + b4["frees"], inc=True)
            self.produce(b4, rvp)
            tcw = self.t_cw
            rv = self.eop("scalar", lambda e: e.copy(
                out=h["cw_s"][:, :], in_=h["banks"][4][:, 0:TT]),
                waits=b4["ready"] + tcw["frees"])
            self.consume(b4, rv)
            self.produce(tcw, rv)
            if ei == E - 1:
                self.consume(tc, rv)
            rvh = None
            for j in range(NU):
                bg, bu = gu_unit(d["wgu"][ei * 2 * NU:(ei + 1) * 2 * NU], j)
                rvh = unit_epilogue(j, use_cw=True, bg=bg, bu=bu)
                if j == NU - 1:
                    self.consume(tcw, rvh)
                yield
            self.produce(self.t_h, rvh)
            yield from down_pass(
                d["wdn"][ei * (NC // 2):(ei + 1) * (NC // 2)], "add")
            last = self.last_down
            self.t_h["frees"] = [last]
            self.acc_dep = [last]

        rv = self.dma("sync",
                      d["acc_dram"].rearrange("(c p) l -> p c l", p=P)
                      [:, :, col0:col0 + TT],
                      h["acc_s"][:, :].rearrange("p (c l) -> p c l", c=NC),
                      "accst", waits=self.now("dve") + acc_sc + self.acc_dep)
        self.accst_done[s] = rv
        tacc["frees"] = [rv]
        self.acc_dep = []
        txnf["frees"] = [("pe", self.cnt["pe"])]

    def emit_fin_st(self, s):
        h, d, cfg = self.h, self.d, self.cfg
        D, TT = cfg["D"], cfg["TT"]
        NC = D // P
        col0 = s * TT
        yws = self.yw_done[s]
        for c in range(NC):
            sl = 0
            ta, tyy, txx = self.t_fa[0], self.t_fy[0], self.t_fx[0]
            rva = self.dma("gpsimd", h["fa_sl"][0][:, :],
                           d["acc_dram"][c * P:(c + 1) * P, col0:col0 + TT],
                           "fa0", waits=ta["frees"] + [self.accst_done[s]])
            self.produce(ta, rva)
            rvy = self.dma("gpsimd", h["fy_sl"][0][:, :],
                           d["y_dram"][c * P:(c + 1) * P, col0:col0 + TT],
                           "fy0", waits=tyy["frees"] + yws)
            self.produce(tyy, rvy)
            rvx = self.dma("gpsimd", h["fx_sl"][0][:, :],
                           d["x_own"][c * P:(c + 1) * P, col0:col0 + TT],
                           "fx0", waits=txx["frees"])
            self.produce(txx, rvx)
            rv1 = self.eop("vector", lambda e, sl=sl: e.tensor_add(
                h["fa_sl"][sl][:, :], h["fa_sl"][sl][:, :],
                h["fy_sl"][0][:, :]),
                waits=ta["ready"] + tyy["ready"])
            self.consume(tyy, rv1)
            rv2 = self.eop("vector", lambda e, sl=sl: e.tensor_add(
                h["fa_sl"][sl][:, :], h["fa_sl"][sl][:, :],
                h["fx_sl"][0][:, :]), waits=txx["ready"])
            self.consume(txx, rv2)
            rvo = self.dma("gpsimd",
                           d["out_fm"][c * P:(c + 1) * P, col0:col0 + TT],
                           h["fa_sl"][0][:, :], "outst0", waits=[rv2])
            self.consume(ta, rvo)
            ta["frees"].append(rvo)

    def emit_end(self):
        self.wait("gpsimd", "outst0", self.cnt["outst0"])


def build_program(cfg):
    _check(cfg)
    nc = bass.Bass("TRN2", target_bir_lowering=False)
    Builder(nc, cfg).build()
    return nc


def _pair_cols(m, inter):
    nj = inter // P
    perm = np.empty(2 * inter, np.int64)
    for j in range(nj):
        perm[j * 2 * P:j * 2 * P + P] = np.arange(j * P, (j + 1) * P)
        perm[j * 2 * P + P:(j + 1) * 2 * P] = inter + np.arange(j * P, (j + 1) * P)
    return m[:, perm]


def _prep_shared(w, cfg):
    D, I, SI, E = cfg["D"], cfg["I"], cfg["SI"], cfg["E"]
    NS = cfg["NS"]
    NC = D // P
    out = {}
    out["wxp"] = np.ascontiguousarray(w["L0_attn_W_xproj"].T).astype(BF)
    out["wdt"] = np.ascontiguousarray(w["L0_attn_W_dt"].T).astype(BF)
    out["alog"] = np.ascontiguousarray(
        w["L0_attn_A_log"].reshape(NC, P, NS).transpose(1, 0, 2)
        .reshape(P, NC * NS)).astype(np.float32)
    out["bdt"] = np.ascontiguousarray(
        w["L0_attn_b_dt"].reshape(NC, P).T).astype(np.float32)
    out["dpar"] = np.ascontiguousarray(
        w["L0_attn_D_param"].reshape(NC, P).T).astype(np.float32)
    out["gate_l"] = np.ascontiguousarray(
        w["L0_ffn_gate_w"].T.reshape(NC, P, E).transpose(1, 0, 2)
        .reshape(P, NC * E)).astype(BF)
    NU = I // P
    NCG = D // (2 * P)
    NQ = SI // I
    gu = w["L0_ffn_expert_gate_up"]
    wgu = np.empty((E, 2 * NU, P, NC * P), BF)
    for e in range(E):
        pw = _pair_cols(gu[e].T, I)
        wgu[e] = pw.reshape(NC, P, 2 * NU, P).transpose(2, 1, 0, 3) \
            .reshape(2 * NU, P, NC * P).astype(BF)
    out["wgu"] = wgu.reshape(E * 2 * NU, P, NC * P)
    wdn = np.empty((E, NCG, P, NU * 2 * P), BF)
    for e in range(E):
        dw = w["L0_ffn_expert_down"][e].T
        wdn[e] = dw.reshape(NU, P, NCG, 2 * P).transpose(2, 1, 0, 3) \
            .reshape(NCG, P, NU * 2 * P).astype(BF)
    out["wdn"] = wdn.reshape(E * NCG, P, NU * 2 * P)
    ps = _pair_cols(np.ascontiguousarray(w["L0_ffn_shared_gate_up"].T), SI)
    out["sgu"] = np.ascontiguousarray(
        ps.reshape(NC, P, 2 * SI // P, P).transpose(2, 1, 0, 3)
        .reshape(2 * SI // P, P, NC * P)).astype(BF)
    sd = np.ascontiguousarray(w["L0_ffn_shared_down"].T)
    out["sdn"] = np.ascontiguousarray(
        sd.reshape(NQ, NU, P, NCG, 2 * P).transpose(0, 3, 2, 1, 4)
        .reshape(NQ * NCG, P, NU * 2 * P)).astype(BF)
    out["sgv_l"] = np.ascontiguousarray(
        w["L0_ffn_shared_gate_vec"].reshape(NC, P).T).astype(BF)
    e16 = np.zeros((16, 16 * P), np.float32)
    for j in range(16):
        e16[j, j * P:(j + 1) * P] = 1.0
    out["e16"] = e16.astype(BF)
    out["ident"] = np.eye(P, dtype=np.float32)
    out["onesb"] = np.ones((P, 1), BF)
    out["onesr"] = np.ones((1, P), np.float32)
    return out


def kernel(L0_attn_x, L0_attn_W_xproj, L0_attn_W_dt, L0_attn_b_dt,
           L0_attn_A_log, L0_attn_D_param, L0_ffn_gate_w,
           L0_ffn_expert_gate_up, L0_ffn_expert_down, L0_ffn_shared_gate_up,
           L0_ffn_shared_down, L0_ffn_shared_gate_vec):
    cfg = dict(CFG_FULL)
    TOK, NPREV = cfg["TOK"], cfg["NPREV"]
    x = np.asarray(L0_attn_x, np.float32)
    B, L, D = x.shape
    ncore = 8
    nq = ncore // B
    assert L == nq * TOK and D == cfg["D"]
    w = {k: np.asarray(v, np.float32) for k, v in dict(
        L0_attn_W_xproj=L0_attn_W_xproj, L0_attn_W_dt=L0_attn_W_dt,
        L0_attn_b_dt=L0_attn_b_dt, L0_attn_A_log=L0_attn_A_log,
        L0_attn_D_param=L0_attn_D_param, L0_ffn_gate_w=L0_ffn_gate_w,
        L0_ffn_expert_gate_up=L0_ffn_expert_gate_up,
        L0_ffn_expert_down=L0_ffn_expert_down,
        L0_ffn_shared_gate_up=L0_ffn_shared_gate_up,
        L0_ffn_shared_down=L0_ffn_shared_down,
        L0_ffn_shared_gate_vec=L0_ffn_shared_gate_vec).items()}
    shared = _prep_shared(w, cfg)

    nc = build_program(cfg)
    in_maps = []
    for core in range(ncore):
        b, q = core // nq, core % nq
        xo = np.ascontiguousarray(x[b, q * TOK:(q + 1) * TOK, :].T)
        xp = np.empty((D, NPREV * TOK), np.float32)
        for r in range(NPREV):
            sq = max(q - NPREV + r, 0)
            xp[:, r * TOK:(r + 1) * TOK] = x[b, sq * TOK:(sq + 1) * TOK, :].T
        sel = np.zeros((P, 2 * NPREV), np.float32)
        for r in range(NPREV):
            active = (q - NPREV + r) >= 0
            sel[:, r] = 1.0 if active else 0.0
            sel[:, NPREV + r] = 0.0 if active else 1.0
        m = dict(shared)
        m["x_own"] = xo
        m["x_prev"] = np.ascontiguousarray(xp)
        m["selin"] = sel
        in_maps.append(m)

    res = run_bass_kernel_spmd(nc, in_maps, core_ids=list(range(ncore)))
    out = np.empty((B, L, D), np.float32)
    for core in range(ncore):
        b, q = core // nq, core % nq
        out[b, q * TOK:(q + 1) * TOK, :] = res.results[core]["out_fm"].T
    return out
